# version 01
# baseline (speedup 1.0000x reference)
import numpy as np
import concourse.bass as bass
import concourse.mybir as mybir
from concourse.bass_utils import run_bass_kernel_spmd

F32 = mybir.dt.float32
BF16 = mybir.dt.bfloat16
I32 = mybir.dt.int32
ALU = mybir.AluOpType
AF = mybir.ActivationFunctionType
AX = mybir.AxisListType

B, T, C, H, D = 4, 2048, 2048, 16, 128
HID = 4 * C
NT = 16
LN_EPS = 1e-5
RMS_EPS = 1.1920929e-07
EXP_SHIFT = -11.5
DEAD = -1e9
SCALE = float(1.0 / np.sqrt(np.float32(128.0)))
TWO_PI = float(2 * np.pi)
KB = 1024

COMBOS = [(qb, c) for qb in (0, 1) for c in range(12 if qb == 0 else 16)]
NCOMB = len(COMBOS)
CIDX = {qc: i for i, qc in enumerate(COMBOS)}


class Ctx:

    def __init__(self, nc):
        self.nc = nc
        self.sem = {}
        self.cnt = {}
        for name in ("LS", "LG", "PE", "DV", "AC", "GP"):
            self.sem[name] = nc.alloc_semaphore(name)
            self.cnt[name] = 0

    def inc(self, inst, s, n=1):
        inst.then_inc(self.sem[s], n)
        self.cnt[s] += n
        return self.cnt[s]

    def w(self, eng, s, val=None):
        if val is None:
            val = self.cnt[s]
        eng.wait_ge(self.sem[s], val)

    def dma_s(self, out, in_):
        return self.inc(self.nc.sync.dma_start(out=out, in_=in_), "LS", 16)

    def dma_g(self, out, in_):
        return self.inc(self.nc.gpsimd.dma_start(out=out, in_=in_), "LG", 16)


def build(debug_taps=(), stop_phase=None):
    nc = bass.Bass("TRN2", target_bir_lowering=False, debug=False,
                   detect_race_conditions=False)
    ctx = Ctx(nc)
    v, sc, te, g, sy = nc.vector, nc.scalar, nc.tensor, nc.gpsimd, nc.sync

    xb = nc.dram_tensor("xb", [T, C], F32, kind="ExternalInput").ap()
    wqkvT = nc.dram_tensor("wqkvT", [C, 3 * C], F32, kind="ExternalInput").ap()
    woT = nc.dram_tensor("woT", [C, C], F32, kind="ExternalInput").ap()
    gwT = nc.dram_tensor("gwT", [C, HID], F32, kind="ExternalInput").ap()
    vwT = nc.dram_tensor("vwT", [C, HID], F32, kind="ExternalInput").ap()
    owT = nc.dram_tensor("owT", [HID, C], F32, kind="ExternalInput").ap()
    wmod = nc.dram_tensor("wmod", [C, 6 * C], F32, kind="ExternalInput").ap()
    ccol = nc.dram_tensor("ccol", [128, 16], F32, kind="ExternalInput").ap()
    gb_in = nc.dram_tensor("gb_in", [1, HID], F32, kind="ExternalInput").ap()
    vb_in = nc.dram_tensor("vb_in", [1, HID], F32, kind="ExternalInput").ap()
    ob_in = nc.dram_tensor("ob_in", [1, C], F32, kind="ExternalInput").ap()
    biastab = nc.dram_tensor("biastab", [128, NCOMB], F32,
                             kind="ExternalInput").ap()
    pos_in = nc.dram_tensor("pos_col", [128, 16], F32,
                            kind="ExternalInput").ap()
    identb_in = nc.dram_tensor("identb", [128, 128], BF16,
                               kind="ExternalInput").ap()
    identf_in = nc.dram_tensor("identf", [128, 128], F32,
                               kind="ExternalInput").ap()

    y_out = nc.dram_tensor("y", [1024, C], F32, kind="ExternalOutput").ap()
    gv_hbm = nc.dram_tensor("gv_hbm", [1024, HID], F32, kind="Internal").ap()
    xnew_hbm = nc.dram_tensor("xnew_hbm", [1024, C], F32, kind="Internal").ap()

    taps = {}

    def tap(name, shape, dtype=F32):
        if name in debug_taps:
            taps[name] = nc.dram_tensor("tap_" + name, shape, dtype,
                                        kind="ExternalOutput").ap()
        return taps.get(name)

    ARENA_SZ = 204 * KB
    arena = nc.alloc_sbuf_tensor("arena", [128, ARENA_SZ], mybir.dt.uint8)
    A0 = nc.lookup_mloc(arena).addr

    _static_iv = []

    def place(name, shape, dtype, off):
        h_ = nc.alloc_sbuf_tensor_at(name, shape, dtype, offset=A0 + off)
        nbytes = int(np.prod(shape[1:])) * mybir.dt.size(dtype)
        assert off + nbytes <= ARENA_SZ, (name, off, nbytes)
        if off + nbytes <= 68 * KB:
            for o0, o1, n0 in _static_iv:
                assert off >= o1 or off + nbytes <= o0, (name, n0, off)
            _static_iv.append((off, off + nbytes, name))
        return h_

    uf = place("uf", [128, 2048], F32, 0)
    u32 = place("u32", [128, 512], I32, 8 * KB)
    ef = place("ef", [128, 512], F32, 10 * KB)
    trash_bf = place("trash_bf", [128, 2048], BF16, 12 * KB)
    mbf = place("mbf", [1, 4 * C], BF16, 16 * KB)
    sin_tm = place("sin_tm", [128, 1024], BF16, 32 * KB)
    cos_tm = place("cos_tm", [128, 1024], BF16, 34 * KB)
    gstr = [place(f"gstr{i}", [128, 2048], BF16, 36 * KB + i * 4 * KB)
            for i in range(2)]
    brow_o = place("brow_o", [1, C], BF16, 44 * KB)
    exp_sb = [place(f"exp{i}", [128, 512], BF16, 48 * KB + i * KB)
              for i in range(2)]
    inv_sb = place("inv_sb", [1, 512], F32, 50 * KB)
    inv_b = place("inv_b", [128, 512], BF16, 52 * KB)
    gate_s = place("gate_s", [128, 512], F32, 53 * KB)
    gv_s = place("gv_s", [128, 512], F32, 55 * KB)
    xq = [place(f"xq{i}", [128, 512], F32, 57 * KB + i * 2 * KB)
          for i in range(2)]
    wq_a = place("wq_a", [128, 512], BF16, 61 * KB)
    wq_b = place("wq_b", [128, 512], BF16, 62 * KB)
    ident = place("ident_sb", [128, 128], BF16, 63 * KB)
    identf = place("identf_sb", [128, 128], F32, 63 * KB + 256)
    tri = place("tri", [128, 128], BF16, 63 * KB + 768)
    btab = place("btab", [128, NCOMB], F32, 64 * KB)
    pcol = place("pcol", [128, 16], F32, 64 * KB + 128)
    cc_sb = place("cc_sb", [128, 16], F32, 64 * KB + 192)
    silu_c = place("silu_c", [128, 16], BF16, 64 * KB + 256)
    silu_cf = nc.alloc_sbuf_tensor_at("silu_cf", [128, 16], F32,
                                      offset=A0 + 8 * KB + 1024)
    invfq = place("invfq", [1, 64], F32, 64 * KB + 288)
    ivb = place("ivb", [128, 64], F32, 64 * KB + 544)
    gabs = place("gabs", [128, 192], F32, 64 * KB + 800)
    grow = place("grow", [1, 16], F32, 64 * KB + 1568)
    gam = place("gam", [128, 8], F32, 64 * KB + 1632)
    gvmax = place("gvmax", [128, 8], F32, 64 * KB + 1664)
    s2c = place("s2c", [128, 8], F32, 64 * KB + 1696)
    is2c = place("is2c", [128, 8], F32, 64 * KB + 1728)
    rms_c = place("rms_c", [128, 12], F32, 64 * KB + 1760)
    ones_col_bf = place("ones_col_bf", [128, 1], BF16, 64 * KB + 1824)
    ones_col_f = place("ones_col_f", [128, 1], F32, 64 * KB + 1856)
    ones_row_bf = place("ones_row_bf", [1, 128], BF16, 64 * KB + 1888)
    ones_row_f = place("ones_row_f", [1, 128], F32, 64 * KB + 2144)
    col = {}
    for i, name in enumerate(("sum", "mean", "var", "invs", "half", "sc",
                              "c2pi", "gmax", "srow", "invsr", "tmp")):
        col[name] = place("col_" + name, [128, 2], F32,
                          64 * KB + 2688 + i * 32)
    dm3 = place("dm3", [128, 512], BF16, 64 * KB + 3040)

    B0 = 68 * KB
    mrow = place("mrow", [1, 6 * C], F32, B0)
    bufA = [place(f"bufA{i}", [128, 2048], BF16, B0 + i * 4 * KB)
            for i in range(16)]
    attn_fm = [place(f"attnfm{i}", [128, 1024], BF16,
                     B0 + 64 * KB + i * 2 * KB) for i in range(16)]
    x_t = [place(f"x_t{i}", [128, 2048], F32, B0 + 64 * KB + i * 8 * KB)
           for i in range(2)]
    bcA1 = place("bcA1", [128, 2048], F32, B0 + 96 * KB)
    bcB1 = place("bcB1", [128, 2048], F32, B0 + 104 * KB)
    wslice = place("wslice", [128, 4096], BF16, B0 + 96 * KB)
    q_tm = place("q_tm", [128, 2048], BF16, B0 + 104 * KB)
    k_tm = place("k_tm", [128, 4096], BF16, B0 + 108 * KB)
    v_tm = place("v_tm", [128, 4096], BF16, B0 + 116 * KB)
    q_fm = [place(f"qfm{i}", [128, 1024], BF16, B0 + 124 * KB + i * 2 * KB)
            for i in range(2)]
    k_fm = [place(f"kfm{i}", [128, 2048], BF16, B0 + 128 * KB + i * 4 * KB)
            for i in range(2)]
    xnew_t = [place(f"xnew{i}", [128, 2048], F32, B0 + 96 * KB + i * 8 * KB)
              for i in range(2)]
    xo_t = place("xo_t", [128, 2048], F32, B0 + 112 * KB)
    bc2 = place("bc2", [128, 2048], F32, B0 + 120 * KB)
    bcA2 = place("bcA2", [128, 2048], F32, B0 + 120 * KB)
    bcB2 = place("bcB2", [128, 2048], F32, B0 + 128 * KB)
    gw_bf = place("gw_bf", [128, 8192], BF16, B0 + 96 * KB)
    vw_bf = place("vw_bf", [128, 8192], BF16, B0 + 112 * KB)
    wg_op = place("wg_op", [128, 8192], BF16, B0 + 0)
    wv_op = place("wv_op", [128, 8192], BF16, B0 + 16 * KB)
    brow_g = place("brow_g", [1, HID], BF16, B0 + 32 * KB)
    brow_v = place("brow_v", [1, HID], BF16, B0 + 48 * KB)
    acc = [place(f"acc{i}", [128, 2048], F32, B0 + 32 * KB + i * 8 * KB)
           for i in range(8)]
    wq_col = place("wq_col", [128, 8192], BF16, B0 + 96 * KB)

    psum = [nc.alloc_psum_tensor(f"pb{i}", [128, 512], F32) for i in range(6)]
    prow = nc.alloc_psum_tensor("prow", [1, 512], F32)
    ptr = nc.alloc_psum_tensor("ptr", [128, 1024], BF16)

    h1_fm = bufA
    wo_sb = bufA
    q_h2 = attn_fm

    def vf(inst):
        ctx.w(v, "DV", ctx.inc(inst, "DV"))

    def _stop(ph):
        if stop_phase == ph:
            sy.wait_ge(ctx.sem["LS"], ctx.cnt["LS"])
            return True
        return False


    ctx.dma_s(ident[:], identb_in)
    ctx.dma_s(identf[:], identf_in)
    ctx.dma_s(btab[:], biastab)
    ctx.dma_s(pcol[:], pos_in)
    ld0 = ctx.dma_s(cc_sb[:], ccol)
    ctx.dma_g(brow_o[:], ob_in)

    v.memset(ones_col_bf[:], 1.0)
    v.memset(ones_col_f[:], 1.0)
    v.memset(ones_row_bf[:], 1.0)
    v.memset(ones_row_f[:], 1.0)
    v.memset(col["half"][:], 0.5)
    v.memset(col["sc"][:], SCALE)
    v.memset(col["c2pi"][:], TWO_PI)
    dv_ms = ctx.inc(v.memset(trash_bf[:], 0.0), "DV")

    g.memset(tri[:], 1.0)
    gp_t = ctx.inc(g.engine_nop(), "GP")
    g.wait_ge(ctx.sem["GP"], gp_t)
    g.affine_select(tri[:], tri[:], pattern=[[1, 128]],
                    compare_op=ALU.is_ge, fill=0.0, base=0,
                    channel_multiplier=-1)
    g.iota(invfq[:], pattern=[[1, 64]], base=0, channel_multiplier=0,
           allow_small_or_imprecise_dtypes=True)
    gp0 = ctx.inc(g.engine_nop(), "GP")

    ctx.w(v, "GP", gp0)
    vf(v.memset(dm3[:], 0.0))
    vf(v.tensor_copy(out=dm3[:, 384:512], in_=tri[:]))
    v.memset(col["tmp"][:], float(-np.log(10000.0) / 64.0))
    dv0 = ctx.inc(v.engine_nop(), "DV")
    ctx.w(sc, "GP", gp0)
    ctx.w(sc, "DV", dv0)
    ac0 = ctx.inc(sc.activation(out=invfq[:], in_=invfq[:], func=AF.Exp,
                                scale=col["tmp"][0:1, 0:1]), "AC")
    ctx.w(te, "AC", ac0)
    pe0 = ctx.inc(te.matmul(psum[0][:, 0:64], ones_row_f[:], invfq[:],
                            start=True, stop=True), "PE")
    ctx.w(v, "PE", pe0)
    ctx.w(v, "LS", ld0)
    dv1 = ctx.inc(v.tensor_copy(out=ivb[:], in_=psum[0][:, 0:64]), "DV")

    def emit_sincos(dst, s, phase_add):
        ctx.w(v, "AC")
        v.tensor_scalar(out=uf[:, 64:128], in0=ivb[:],
                        scalar1=pcol[:, s:s + 1], scalar2=None, op0=ALU.mult)
        v.tensor_scalar(out=uf[:, 0:64], in0=uf[:, 64:128],
                        scalar1=float(1.0 / TWO_PI), scalar2=phase_add,
                        op0=ALU.mult, op1=ALU.add)
        v.tensor_copy(out=u32[:, 0:64], in_=uf[:, 0:64])
        v.tensor_tensor(out=ef[:, 0:64], in0=uf[:, 0:64], in1=u32[:, 0:64],
                        op=ALU.subtract)
        v.scalar_tensor_tensor(out=ef[:, 0:64], in0=ef[:, 0:64],
                               scalar=col["half"][:, 0:1], in1=u32[:, 0:64],
                               op0=ALU.is_ge, op1=ALU.add)
        dvx = ctx.inc(v.tensor_tensor(out=uf[:, 0:64], in0=uf[:, 0:64],
                                      in1=ef[:, 0:64], op=ALU.subtract),
                      "DV")
        ctx.w(sc, "DV", dvx)
        return ctx.inc(sc.activation(out=dst, in_=uf[:, 0:64], func=AF.Sin,
                                     scale=col["c2pi"][:, 0:1]), "AC")

    ctx.w(v, "DV", dv1)
    ac_tab = 0
    for s in range(16):
        emit_sincos(sin_tm[:, s * 64:(s + 1) * 64], s, 0.0)
        ac_tab = emit_sincos(cos_tm[:, s * 64:(s + 1) * 64], s, 0.25)

    t = tap("sin", [128, 1024], BF16)
    if t is not None:
        ctx.w(sy, "AC", ac_tab)
        ctx.dma_s(t, sin_tm[:])
    t = tap("cos", [128, 1024], BF16)
    if t is not None:
        ctx.w(sy, "AC", ac_tab)
        ctx.dma_s(t, cos_tm[:])

    if _stop("tables"):
        return nc, taps

    ctx.w(sc, "LS", ld0)
    sc.activation(out=silu_c[:], in_=cc_sb[:], func=AF.Silu)
    ac_sc = ctx.inc(sc.activation(out=silu_cf[:], in_=cc_sb[:],
                                  func=AF.Silu), "AC")
    ctx.w(te, "AC", ac_sc)
    ctx.w(te, "DV")
    pe_tile = {}
    dv_mod = 0
    for jb in range(6):
        gate_jb = jb in (2, 5)
        for ck in range(16):
            i = jb * 16 + ck
            wb = x_t[i % 2] if gate_jb else gstr[i % 2]
            lhs_c = silu_cf if gate_jb else silu_c
            if i >= 2:
                ctx.w(g, "PE", pe_tile[i - 2])
            tgt = ctx.dma_g(wb[:], wmod[ck * 128:(ck + 1) * 128,
                                        jb * 2048:(jb + 1) * 2048])
            ctx.w(te, "LG", tgt)
            for q4 in range(4):
                mm_ = te.matmul(psum[q4][0:1, :], lhs_c[:, ck:ck + 1],
                                wb[:, q4 * 512:(q4 + 1) * 512],
                                start=(ck == 0), stop=(ck == 15))
            pe_tile[i] = ctx.inc(mm_, "PE")
        ctx.w(v, "PE", pe_tile[jb * 16 + 15])
        for q4 in range(4):
            off = jb * 2048 + q4 * 512
            dv_mod = ctx.inc(v.tensor_copy(out=mrow[:, off:off + 512],
                                           in_=psum[q4][0:1, :]), "DV")
        ctx.w(te, "DV", dv_mod)

    SA, CA, GA, SB, CB, GB = (i * C for i in range(6))
    v.tensor_scalar(out=mrow[:, CA:CA + C], in0=mrow[:, CA:CA + C],
                    scalar1=1.0, scalar2=None, op0=ALU.add)
    v.tensor_scalar(out=mrow[:, CB:CB + C], in0=mrow[:, CB:CB + C],
                    scalar1=1.0, scalar2=None, op0=ALU.add)
    for off in (GA, GB):
        sl = mrow[:, off:off + C]
        v.tensor_scalar(out=uf[0:1, 0:C], in0=sl, scalar1=-1.0,
                        scalar2=0.25, op0=ALU.mult, op1=ALU.max)
        v.scalar_tensor_tensor(out=uf[0:1, 0:C], in0=sl, scalar=0.25,
                               in1=uf[0:1, 0:C], op0=ALU.max, op1=ALU.max)
        v.tensor_scalar(out=x_t[0][0:1, 0:C], in0=sl, scalar1=0.0,
                        scalar2=None, op0=ALU.is_ge)
        v.tensor_scalar(out=x_t[0][0:1, 0:C], in0=x_t[0][0:1, 0:C],
                        scalar1=2.0, scalar2=-1.0, op0=ALU.mult, op1=ALU.add)
        v.tensor_tensor(out=sl, in0=uf[0:1, 0:C], in1=x_t[0][0:1, 0:C],
                        op=ALU.mult)
    v.tensor_copy(out=mbf[:, 0:C], in_=mrow[:, CB:CB + C])
    v.tensor_copy(out=mbf[:, C:2 * C], in_=mrow[:, SB:SB + C])
    v.tensor_copy(out=mbf[:, 2 * C:3 * C], in_=mrow[:, GA:GA + C])
    dv_m = ctx.inc(v.tensor_copy(out=mbf[:, 3 * C:4 * C],
                                 in_=mrow[:, GB:GB + C]), "DV")
    t = tap("mod", [1, 6 * C])
    if t is not None:
        ctx.w(sy, "DV", dv_m)
        ctx.dma_s(t, mrow[:])

    def bcast_row(dst, src_row, ones):
        for ocn in range(4):
            ctx.w(te, "DV")
            mm_ = te.matmul(psum[4 + ocn % 2][:], ones[:],
                            src_row[:, ocn * 512:(ocn + 1) * 512],
                            start=True, stop=True)
            pe_b = ctx.inc(mm_, "PE")
            ctx.w(v, "PE", pe_b)
            ctx.inc(v.tensor_copy(out=dst[:, ocn * 512:(ocn + 1) * 512],
                                  in_=psum[4 + ocn % 2][:]), "DV")

    t = tap("mbft", [1, 4 * C], BF16)
    if t is not None:
        ctx.w(sy, "DV", dv_m)
        ctx.dma_s(t, mbf[:])
    ctx.w(te, "DV", dv_m)
    bcast_row(bcA1, mrow[:, CA:CA + C], ones_row_f)
    bcast_row(bcB1, mrow[:, SA:SA + C], ones_row_f)

    if _stop("mod"):
        return nc, taps

    def emit_ln_apply(src_tile, scale_b, shift_b, out_bf):
        vf(v.reduce_sum(col["sum"][:, 0:1], src_tile, axis=AX.X))
        vf(v.tensor_scalar(out=col["mean"][:, 0:1], in0=col["sum"][:, 0:1],
                           scalar1=1.0 / 2048.0, scalar2=None, op0=ALU.mult))
        vf(v.scalar_tensor_tensor(out=trash_bf[:], in0=src_tile,
                                  scalar=col["mean"][:, 0:1], in1=src_tile,
                                  op0=ALU.subtract, op1=ALU.mult,
                                  accum_out=col["var"][:, 0:1]))
        vf(v.tensor_scalar(out=col["var"][:, 0:1], in0=col["var"][:, 0:1],
                           scalar1=1.0 / 2048.0, scalar2=LN_EPS,
                           op0=ALU.mult, op1=ALU.add))
        dvs = ctx.inc(v.reciprocal(out=col["var"][:, 1:2],
                                   in_=col["var"][:, 0:1]), "DV")
        ctx.w(sc, "DV", dvs)
        acs = ctx.inc(sc.activation(out=col["invs"][:, 0:1],
                                    in_=col["var"][:, 1:2], func=AF.Sqrt),
                      "AC")
        ctx.w(v, "AC", acs)
        v.tensor_scalar(out=uf[:], in0=src_tile, scalar1=col["mean"][:, 0:1],
                        scalar2=col["invs"][:, 0:1], op0=ALU.subtract,
                        op1=ALU.mult)
        v.tensor_tensor(out=uf[:], in0=uf[:], in1=scale_b[:], op=ALU.mult)
        return ctx.inc(v.tensor_tensor(out=out_bf, in0=uf[:], in1=shift_b[:],
                                       op=ALU.add), "DV")

    def emit_transpose16(src_bf, dst_fn, dst_col):
        last_dv = 0
        for cg in range(2):
            ctx.w(te, "DV")
            for c8 in range(8):
                cch = cg * 8 + c8
                mm_ = te.transpose(ptr[:, c8 * 128:(c8 + 1) * 128],
                                   src_bf[:, cch * 128:(cch + 1) * 128],
                                   ident[:])
            pe_t = ctx.inc(mm_, "PE")
            ctx.w(v, "PE", pe_t)
            for c8 in range(8):
                cch = cg * 8 + c8
                last = v.tensor_copy(
                    out=dst_fn(cch)[:, dst_col:dst_col + 128],
                    in_=ptr[:, c8 * 128:(c8 + 1) * 128])
            last_dv = ctx.inc(last, "DV")
        return last_dv

    dv_h1 = {}
    ctx.w(sy, "PE")
    ctx.w(sy, "DV")
    for s in range(NT):
        xt = x_t[s % 2]
        if s >= 2:
            ctx.w(sy, "DV", dv_h1[s - 2])
        tgt = ctx.dma_s(xt[:], xb[s * 128:(s + 1) * 128, :])
        ctx.w(v, "LS", tgt)
        dvln = emit_ln_apply(xt[:], bcA1, bcB1, trash_bf[:])
        if s == 0:
            t0 = tap("h1tm", [128, 2048], BF16)
            if t0 is not None:
                ctx.w(sy, "DV", dvln)
                ctx.dma_s(t0, trash_bf[:])
            t0 = tap("x0", [128, 2048])
            if t0 is not None:
                ctx.w(sy, "DV", dvln)
                ctx.dma_s(t0, xt[:])
        dv_h1[s] = emit_transpose16(trash_bf, lambda cch: h1_fm[cch],
                                    s * 128)

    t = tap("h1fm", [128, 2048 * 16], BF16)
    if t is not None:
        ctx.w(sy, "DV", dv_h1[NT - 1])
        for cch in range(16):
            ctx.dma_s(t[:, cch * 2048:(cch + 1) * 2048], h1_fm[cch][:])

    if _stop("h1"):
        return nc, taps

    gamma_pieces = []
    for i in range(16):
        for p in range(4):
            gamma_pieces.append((gwT[i * 128:(i + 1) * 128,
                                     p * 2048:(p + 1) * 2048], i * 4 + p))
    for i in range(16):
        for p in range(4):
            gamma_pieces.append((vwT[i * 128:(i + 1) * 128,
                                     p * 2048:(p + 1) * 2048],
                                 64 + i * 4 + p))
    for i in range(64):
        gamma_pieces.append((owT[i * 128:(i + 1) * 128, :], 128 + i))
    gamma_ac = {}
    gamma_k = [0]

    def emit_gamma(n):
        for _ in range(n):
            k = gamma_k[0]
            if k >= len(gamma_pieces):
                return
            gamma_k[0] = k + 1
            src, gcol = gamma_pieces[k]
            wb = gstr[k % 2]
            if k >= 2:
                ctx.w(g, "AC", gamma_ac[k - 2])
            else:
                ctx.w(g, "PE")
            tgt = ctx.dma_g(wb[:], src)
            ctx.w(sc, "LG", tgt)
            gamma_ac[k] = ctx.inc(
                sc.activation(out=trash_bf[:], in_=wb[:], func=AF.Abs,
                              accum_out=gabs[:, gcol:gcol + 1]), "AC")

    SECQ, SECK, SECV = 0, 2048, 4096

    for p2 in range(8):
        heads = [p2 * 2, p2 * 2 + 1]
        for sec_i, (sec_off, tiles) in enumerate(
                ((SECQ, range(8, 16)), (SECK, range(16)), (SECV, range(16)))):
            csel = sec_off + p2 * 256
            w3 = wqkvT[:, csel:csel + 256].rearrange("(k p) n -> p k n",
                                                     p=128)
            ctx.w(g, "PE")
            ctx.w(g, "DV")
            tgt = ctx.dma_g(wslice[:], w3)
            ctx.w(te, "LG", tgt)
            for s in tiles:
                pb = psum[s % 2]
                ctx.w(te, "DV")
                for cch in range(16):
                    mm_ = te.matmul(pb[:, 0:256],
                                    h1_fm[cch][:, s * 128:(s + 1) * 128],
                                    wslice[:, cch * 256:(cch + 1) * 256],
                                    start=(cch == 0), stop=(cch == 15))
                pe_mm = ctx.inc(mm_, "PE")
                ctx.w(v, "PE", pe_mm)
                if sec_i == 0:
                    dst_ap = q_tm[:, (s - 8) * 256:(s - 7) * 256]
                elif sec_i == 1:
                    dst_ap = k_tm[:, s * 256:(s + 1) * 256]
                else:
                    dst_ap = v_tm[:, s * 256:(s + 1) * 256]
                ctx.inc(v.tensor_copy(out=dst_ap, in_=pb[:, 0:256]), "DV")

        def rope_rms(buf, s, tile_idx):
            base = tile_idx * 256
            so = s * 64
            sin_s = sin_tm[:, so:so + 64]
            cos_s = cos_tm[:, so:so + 64]
            for hl in range(2):
                lo = buf[:, base + hl * 128: base + hl * 128 + 64]
                hi = buf[:, base + hl * 128 + 64: base + hl * 128 + 128]
                v.tensor_tensor(out=uf[:, 0:64], in0=lo, in1=cos_s,
                                op=ALU.mult)
                v.tensor_tensor(out=uf[:, 64:128], in0=hi, in1=sin_s,
                                op=ALU.mult)
                v.tensor_tensor(out=uf[:, 128:192], in0=hi, in1=cos_s,
                                op=ALU.mult)
                v.tensor_tensor(out=uf[:, 192:256], in0=lo, in1=sin_s,
                                op=ALU.mult)
                v.tensor_tensor(out=lo, in0=uf[:, 0:64], in1=uf[:, 64:128],
                                op=ALU.subtract)
                v.tensor_tensor(out=hi, in0=uf[:, 128:192],
                                in1=uf[:, 192:256], op=ALU.add)
                full = buf[:, base + hl * 128: base + hl * 128 + 128]
                ii = v.scalar_tensor_tensor(
                    out=trash_bf[:, 0:128], in0=full,
                    scalar=1.0, in1=full, op0=ALU.mult, op1=ALU.mult,
                    accum_out=rms_c[:, hl:hl + 1])
            vf(ii)
            vf(v.tensor_scalar(out=rms_c[:, 0:2], in0=rms_c[:, 0:2],
                               scalar1=1.0 / 128.0, scalar2=RMS_EPS,
                               op0=ALU.mult, op1=ALU.add))
            dvs = ctx.inc(v.reciprocal(out=rms_c[:, 2:4],
                                       in_=rms_c[:, 0:2]), "DV")
            ctx.w(sc, "DV", dvs)
            acs = ctx.inc(sc.activation(out=rms_c[:, 4:6], in_=rms_c[:, 2:4],
                                        func=AF.Sqrt), "AC")
            ctx.w(v, "AC", acs)
            for hl in range(2):
                sl = buf[:, base + hl * 128: base + hl * 128 + 128]
                v.tensor_scalar(out=sl, in0=sl,
                                scalar1=rms_c[:, 4 + hl:5 + hl],
                                scalar2=None, op0=ALU.mult)
            return ctx.inc(v.engine_nop(), "DV")

        if p2 == 0:
            t0 = tap("qkv0", [128, 2048 * 3], BF16)
            if t0 is not None:
                ctx.w(sy, "DV")
                ctx.dma_s(t0[:, 0:2048], q_tm[:])
                ctx.dma_s(t0[:, 2048:2048 + 4096], k_tm[:])
                ctx.dma_s(t0[:, 6144 - 2048:6144], v_tm[:, 0:2048])

        for s in range(8, 16):
            rope_rms(q_tm, s, s - 8)
        dv_rope = 0
        for s in range(16):
            dv_rope = rope_rms(k_tm, s, s)

        if p2 == 0:
            t0 = tap("rope0", [128, 2048 * 3], BF16)
            if t0 is not None:
                ctx.w(sy, "DV")
                ctx.dma_s(t0[:, 0:2048], q_tm[:])
                ctx.dma_s(t0[:, 2048:6144], k_tm[:])

        for hl in range(2):
            hglob = heads[hl]
            jobs = [(q_tm, s * 256 + hl * 128, q_fm[hl], s * 128)
                    for s in range(8)]
            jobs += [(k_tm, s * 256 + hl * 128, k_fm[hl], s * 128)
                     for s in range(16)]
            ctx.w(te, "DV", dv_rope)
            for j0 in range(0, len(jobs), 8):
                batch = jobs[j0:j0 + 8]
                ctx.w(te, "DV")
                for bi, (sb_, soff, dbuf, doff) in enumerate(batch):
                    mm_ = te.transpose(ptr[:, bi * 128:(bi + 1) * 128],
                                       sb_[:, soff:soff + 128], ident[:])
                pe_t = ctx.inc(mm_, "PE")
                ctx.w(v, "PE", pe_t)
                for bi, (sb_, soff, dbuf, doff) in enumerate(batch):
                    last = v.tensor_copy(out=dbuf[:, doff:doff + 128],
                                         in_=ptr[:, bi * 128:(bi + 1) * 128])
                ctx.inc(last, "DV")
            dv_tp = ctx.cnt["DV"]
            ctx.w(te, "DV", dv_tp)

            for qb in range(2):
                nck = 12 if qb == 0 else 16
                ac_exp = {}
                pe_use = {}
                ctx.w(sc, "PE")
                ctx.w(te, "AC")
                for c_ in range(nck):
                    k_sl = k_fm[hl][:, c_ * 128:(c_ + 1) * 128]
                    v_sl = v_tm[:, c_ * 256 + hl * 128:
                                c_ * 256 + hl * 128 + 128]
                    ps = psum[2 + c_ % 2]
                    eb = exp_sb[c_ % 2]
                    if c_ >= 2:
                        ctx.w(te, "AC", ac_exp[c_ - 2])
                    mm_ = te.matmul(ps[:], k_sl,
                                    q_fm[hl][:, qb * 512:(qb + 1) * 512],
                                    start=True, stop=True)
                    pe_s = ctx.inc(mm_, "PE")
                    ctx.w(sc, "PE", pe_s)
                    if c_ >= 2:
                        ctx.w(sc, "PE", pe_use[c_ - 2])
                    ci = CIDX[(qb, c_)]
                    if p2 == 0 and hl == 0 and qb == 1 and c_ == 0:
                        t0 = tap("sc0", [128, 512])
                        if t0 is not None:
                            ctx.w(v, "PE", pe_s)
                            dtap = ctx.inc(v.tensor_copy(
                                out=uf[:, 0:512], in_=ps[:]), "DV")
                            ctx.w(sy, "DV", dtap)
                            ctx.dma_s(t0, uf[:, 0:512])
                            ctx.w(sc, "LS")
                    ac_exp[c_] = ctx.inc(
                        sc.activation(out=eb[:], in_=ps[:], func=AF.Exp,
                                      scale=col["sc"][:, 0:1],
                                      bias=btab[:, ci:ci + 1]), "AC")
                    if p2 == 0 and hl == 0 and qb == 1 and c_ == 0:
                        t0 = tap("exp0", [128, 512], BF16)
                        if t0 is not None:
                            ctx.w(sy, "AC", ac_exp[c_])
                            ctx.dma_s(t0, eb[:])
                            ctx.w(te, "LS")
                    Dd = c_ - 8 - 4 * qb
                    if 0 <= Dd <= 3:
                        w_ = (Dd + 1) * 128
                        ctx.w(v, "AC", ac_exp[c_])
                        dv_msk = ctx.inc(v.tensor_tensor(
                            out=eb[:, 0:w_], in0=eb[:, 0:w_],
                            in1=dm3[:, (3 - Dd) * 128:512], op=ALU.mult),
                            "DV")
                        ctx.w(te, "DV", dv_msk)
                    else:
                        ctx.w(te, "AC", ac_exp[c_])
                    st_, sp_ = (c_ == 0), (c_ == nck - 1)
                    te.matmul(psum[4][:], v_sl, eb[:], start=st_, stop=sp_)
                    mm_ = te.matmul(prow[:], ones_col_bf[:], eb[:],
                                    start=st_, stop=sp_)
                    pe_use[c_] = ctx.inc(mm_, "PE")
                ctx.w(v, "PE", pe_use[nck - 1])
                if p2 == 0 and hl == 0 and qb == 1:
                    t0 = tap("pat0", [128, 512])
                    if t0 is not None:
                        dtap = ctx.inc(v.tensor_copy(
                            out=uf[:, 0:512], in_=psum[4][:]), "DV")
                        ctx.w(sy, "DV", dtap)
                        ctx.dma_s(t0, uf[:, 0:512])
                    t0 = tap("prow0", [1, 512])
                    if t0 is not None:
                        dtap = ctx.inc(v.tensor_copy(
                            out=uf[0:1, 512:1024], in_=prow[:]), "DV")
                        ctx.w(sy, "DV", dtap)
                        tgt_t = ctx.dma_s(t0, uf[0:1, 512:1024])
                        ctx.w(v, "LS", tgt_t)
                dv_i = ctx.inc(v.reciprocal(out=inv_sb[:], in_=prow[:]),
                               "DV")
                ctx.w(te, "DV", dv_i)
                pe_b = ctx.inc(te.matmul(psum[5][:], ones_row_f[:],
                                         inv_sb[:], start=True, stop=True),
                               "PE")
                ctx.w(v, "PE", pe_b)
                ctx.inc(v.tensor_copy(out=inv_b[:], in_=psum[5][:]), "DV")
                dv_at = ctx.inc(v.tensor_tensor(
                    out=attn_fm[hglob][:, qb * 512:(qb + 1) * 512],
                    in0=psum[4][:], in1=inv_b[:], op=ALU.mult), "DV")
                ctx.w(te, "DV", dv_at)

        emit_gamma(24)

    t = tap("attnfm", [128, 1024 * 16], BF16)
    if t is not None:
        ctx.w(sy, "DV")
        ctx.w(sy, "AC")
        for i in range(16):
            ctx.dma_s(t[:, i * 1024:(i + 1) * 1024], attn_fm[i][:])

    if _stop("attn"):
        return nc, taps

    emit_gamma(len(gamma_pieces))
    ctx.w(v, "AC")
    v.reduce_sum(col["gmax"][:, 0:1], gabs[:, 0:64], axis=AX.X)
    v.reduce_sum(col["gmax"][:, 1:2], gabs[:, 64:128], axis=AX.X)
    dv_g = ctx.inc(v.reduce_sum(col["tmp"][:, 0:1], gabs[:, 128:192],
                                axis=AX.X), "DV")
    ctx.w(te, "DV", dv_g)
    te.matmul(prow[:, 0:2], ones_col_f[:], col["gmax"][:, 0:2],
              start=True, stop=True)
    pe_g = ctx.inc(te.matmul(prow[:, 2:3], ones_col_f[:], col["tmp"][:, 0:1],
                             start=True, stop=True), "PE")
    ctx.w(v, "PE", pe_g)
    vf(v.tensor_copy(out=grow[:, 0:3], in_=prow[:, 0:3]))
    for i in range(3):
        vf(v.tensor_scalar(out=grow[:, i:i + 1], in0=grow[:, i:i + 1],
                           scalar1=1.0 / (C * HID), scalar2=1e-5,
                           op0=ALU.mult, op1=ALU.max))
    for i in range(3):
        v.tensor_scalar(out=grow[:, 4 + 2 * i:5 + 2 * i],
                        in0=grow[:, i:i + 1], scalar1=0.5, scalar2=None,
                        op0=ALU.mult)
        vf(v.tensor_scalar(out=grow[:, 5 + 2 * i:6 + 2 * i],
                           in0=grow[:, i:i + 1], scalar1=-0.5, scalar2=None,
                           op0=ALU.mult))
    dv_g2 = ctx.inc(v.reciprocal(out=grow[:, 10:13], in_=grow[:, 0:3]), "DV")
    ctx.w(v, "DV", dv_g2)
    ctx.w(te, "DV", dv_g2)
    te.matmul(psum[0][:, 0:6], ones_row_f[:], grow[:, 4:10],
              start=True, stop=True)
    pe_g2 = ctx.inc(te.matmul(psum[0][:, 6:8], ones_row_f[:], grow[:, 0:2],
                              start=True, stop=True), "PE")
    ctx.w(v, "PE", pe_g2)
    dv_g3 = ctx.inc(v.tensor_copy(out=gam[:], in_=psum[0][:, 0:8]), "DV")
    t = tap("gamma", [1, 16])
    if t is not None:
        ctx.w(sy, "DV", dv_g3)
        ctx.dma_s(t, grow[:])

    if _stop("gamma"):
        return nc, taps

    t = tap("mbf4", [1, 4 * C], BF16)
    if t is not None:
        ctx.w(sy, "DV")
        ctx.w(sy, "AC")
        ctx.dma_s(t, mbf[:])
    bcast_row(bc2, mbf[:, 2 * C:3 * C], ones_row_bf)
    ctx.w(g, "PE")
    lg_wo = 0
    for cch in range(16):
        lg_wo = ctx.dma_g(wo_sb[cch][:], woT[cch * 128:(cch + 1) * 128, :])
    pe_gate = ctx.cnt["DV"]
    ctx.w(sy, "PE")
    ctx.w(sy, "AC")
    dv_ep = {}
    spill_ls = {}
    for s in range(8):
        xn = xnew_t[s % 2]
        if s >= 2:
            ctx.w(sy, "DV", dv_ep[s - 2])
        if s >= 1:
            ctx.w(sy, "DV", dv_ep[s - 1])
        tgt_o = ctx.dma_s(xo_t[:], xb[(8 + s) * 128:(9 + s) * 128, :])
        ctx.w(te, "LG", lg_wo)
        for oc in range(4):
            pb = psum[oc % 2]
            ctx.w(te, "DV", pe_gate)
            if s >= 2 and oc == 0:
                ctx.w(v, "LS", spill_ls[s - 2])
            for cch in range(16):
                mm_ = te.matmul(pb[:],
                                attn_fm[cch][:, s * 128:(s + 1) * 128],
                                wo_sb[cch][:, oc * 512:(oc + 1) * 512],
                                start=(cch == 0), stop=(cch == 15))
            pe_mm = ctx.inc(mm_, "PE")
            ctx.w(v, "PE", pe_mm)
            if s == 0 and oc == 0:
                t0 = tap("wops", [128, 512])
                if t0 is not None:
                    dtp = ctx.inc(v.tensor_copy(out=ef[:], in_=pb[:]), "DV")
                    ctx.w(sy, "DV", dtp)
                    ctx.dma_s(t0, ef[:])
                t0 = tap("bc2t", [128, 2048])
                if t0 is not None:
                    ctx.dma_s(t0, bc2[:])
            pe_gate = ctx.inc(v.tensor_tensor(
                out=xn[:, oc * 512:(oc + 1) * 512], in0=pb[:],
                in1=bc2[:, oc * 512:(oc + 1) * 512], op=ALU.mult), "DV")
        ctx.w(v, "LS", tgt_o)
        dv_x = ctx.inc(v.tensor_tensor(out=xn[:], in0=xn[:], in1=xo_t[:],
                                       op=ALU.add), "DV")
        dv_ep[s] = dv_x
        ctx.w(sy, "DV", dv_x)
        spill_ls[s] = ctx.dma_s(xnew_hbm[s * 128:(s + 1) * 128, :], xn[:])

    t = tap("xnew", [1024, C])
    if t is not None:
        ctx.w(sy, "LS", ctx.cnt["LS"])
        ctx.dma_s(t, xnew_hbm)

    if _stop("xnew"):
        return nc, taps

    bcast_row(bcA2, mbf[:, 0:C], ones_row_bf)
    bcast_row(bcB2, mbf[:, C:2 * C], ones_row_bf)
    ctx.w(sy, "LS", ctx.cnt["LS"])
    dv_q2 = {}
    for s in range(8):
        xn = xnew_t[s % 2]
        if s >= 2:
            ctx.w(sy, "DV", dv_q2[s - 2])
        tgt = ctx.dma_s(xn[:], xnew_hbm[s * 128:(s + 1) * 128, :])
        ctx.w(v, "LS", tgt)
        emit_ln_apply(xn[:], bcA2, bcB2, trash_bf[:])
        v.tensor_tensor(out=uf[:], in0=uf[:], in1=bcB2[:], op=ALU.add)
        vf(v.reduce_max(col["gmax"][:, 0:1], uf[:], axis=AX.X,
                        apply_absolute_value=True))
        vf(v.tensor_scalar(out=col["gmax"][:, 0:1], in0=col["gmax"][:, 0:1],
                           scalar1=1e-5, scalar2=None, op0=ALU.max))
        vf(v.reciprocal(out=col["srow"][:, 0:1], in_=col["gmax"][:, 0:1]))
        vf(v.tensor_scalar(out=col["srow"][:, 0:1], in0=col["srow"][:, 0:1],
                           scalar1=127.0, scalar2=None, op0=ALU.mult))
        vf(v.tensor_scalar(out=col["invsr"][:, 0:1], in0=col["gmax"][:, 0:1],
                           scalar1=1.0 / 127.0, scalar2=None, op0=ALU.mult))
        v.tensor_scalar(out=uf[:], in0=uf[:], scalar1=col["srow"][:, 0:1],
                        scalar2=128.0, op0=ALU.mult, op1=ALU.add)
        for ch in range(4):
            usl = uf[:, ch * 512:(ch + 1) * 512]
            v.tensor_copy(out=u32[:], in_=usl)
            v.tensor_tensor(out=ef[:], in0=usl, in1=u32[:], op=ALU.subtract)
            v.scalar_tensor_tensor(out=ef[:], in0=ef[:],
                                   scalar=col["half"][:, 0:1], in1=u32[:],
                                   op0=ALU.is_ge, op1=ALU.add)
            v.tensor_scalar(out=trash_bf[:, ch * 512:(ch + 1) * 512],
                            in0=ef[:], scalar1=-128.0,
                            scalar2=col["invsr"][:, 0:1],
                            op0=ALU.add, op1=ALU.mult)
        ctx.inc(v.engine_nop(), "DV")
        dv_q2[s] = emit_transpose16(trash_bf, lambda cch: q_h2[cch],
                                    s * 128)

    t = tap("qh2", [128, 1024 * 16], BF16)
    if t is not None:
        ctx.w(sy, "DV")
        for i in range(16):
            ctx.dma_s(t[:, i * 1024:(i + 1) * 1024], q_h2[i][:])

    if _stop("qh2"):
        return nc, taps

    ctx.w(g, "PE")
    ctx.w(g, "DV")
    ctx.dma_g(brow_g[:], gb_in)
    lg_b = ctx.dma_g(brow_v[:], vb_in)
    ctx.w(v, "LG", lg_b)
    ctx.w(v, "DV", dv_g2)
    v.tensor_scalar(out=brow_g[:], in0=brow_g[:], scalar1=grow[:, 10:11],
                    scalar2=None, op0=ALU.mult)
    v.tensor_scalar(out=brow_v[:], in0=brow_v[:], scalar1=grow[:, 11:12],
                    scalar2=None, op0=ALU.mult)
    ctx.inc(v.tensor_scalar(out=brow_o[:], in0=brow_o[:],
                            scalar1=grow[:, 12:13], scalar2=None,
                            op0=ALU.mult), "DV")

    def emit_ternary(dst_ap, src_ap, cpos, cneg):
        v.tensor_scalar(out=wq_a[:], in0=src_ap,
                        scalar1=gam[:, cpos:cpos + 1], scalar2=None,
                        op0=ALU.is_ge)
        v.tensor_scalar(out=wq_b[:], in0=src_ap,
                        scalar1=gam[:, cneg:cneg + 1], scalar2=None,
                        op0=ALU.is_le)
        return v.tensor_tensor(out=dst_ap, in0=wq_a[:], in1=wq_b[:],
                               op=ALU.subtract)

    ctx.w(v, "PE")
    pe_gv = {}
    for hc in range(16):
        if hc >= 1:
            ctx.w(g, "PE", pe_gv[hc - 1])
        gsrc = gwT[:, hc * 512:(hc + 1) * 512].rearrange(
            "(k p) n -> p k n", p=128)
        vsrc = vwT[:, hc * 512:(hc + 1) * 512].rearrange(
            "(k p) n -> p k n", p=128)
        ctx.dma_g(gw_bf[:], gsrc)
        lg_w = ctx.dma_g(vw_bf[:], vsrc)
        ctx.w(v, "LG", lg_w)
        ctx.w(v, "PE", pe_gv.get(hc - 1, 0))
        for cch in range(16):
            emit_ternary(wg_op[:, cch * 512:(cch + 1) * 512],
                         gw_bf[:, cch * 512:(cch + 1) * 512], 0, 1)
            last = emit_ternary(wv_op[:, cch * 512:(cch + 1) * 512],
                                vw_bf[:, cch * 512:(cch + 1) * 512], 2, 3)
        dv_wq = ctx.inc(last, "DV")
        ctx.w(te, "DV", dv_wq)
        for s in range(8):
            ctx.w(te, "AC")
            ctx.w(te, "DV")
            for cch in range(16):
                te.matmul(psum[0][:], q_h2[cch][:, s * 128:(s + 1) * 128],
                          wg_op[:, cch * 512:(cch + 1) * 512],
                          start=(cch == 0), stop=False)
            mm_ = te.matmul(psum[0][:], ones_row_bf[:],
                            brow_g[:, hc * 512:(hc + 1) * 512],
                            start=False, stop=True)
            pe_gm = ctx.inc(mm_, "PE")
            ctx.w(sc, "PE", pe_gm)
            ctx.w(sc, "DV")
            acg = ctx.inc(sc.activation(out=gate_s[:], in_=psum[0][:],
                                        func=AF.Silu, scale=gam[:, 6:7]),
                          "AC")
            for cch in range(16):
                te.matmul(psum[1][:], q_h2[cch][:, s * 128:(s + 1) * 128],
                          wv_op[:, cch * 512:(cch + 1) * 512],
                          start=(cch == 0), stop=False)
            mm_ = te.matmul(psum[1][:], ones_row_bf[:],
                            brow_v[:, hc * 512:(hc + 1) * 512],
                            start=False, stop=True)
            pe_vm = ctx.inc(mm_, "PE")
            pe_gv[hc] = pe_vm
            ctx.w(v, "PE", pe_vm)
            ctx.w(v, "AC", acg)
            v.scalar_tensor_tensor(out=gv_s[:], in0=psum[1][:],
                                   scalar=gam[:, 7:8], in1=gate_s[:],
                                   op0=ALU.mult, op1=ALU.mult)
            if hc == 0:
                dvm = ctx.inc(v.reduce_max(gvmax[:, s:s + 1], gv_s[:],
                                           axis=AX.X,
                                           apply_absolute_value=True), "DV")
            else:
                vf(v.reduce_max(col["tmp"][:, 0:1], gv_s[:], axis=AX.X,
                                apply_absolute_value=True))
                dvm = ctx.inc(v.tensor_tensor(out=gvmax[:, s:s + 1],
                                              in0=gvmax[:, s:s + 1],
                                              in1=col["tmp"][:, 0:1],
                                              op=ALU.max), "DV")
            ctx.w(sy, "DV", dvm)
            ctx.dma_s(gv_hbm[s * 128:(s + 1) * 128,
                             hc * 512:(hc + 1) * 512], gv_s[:])
            ctx.w(sc, "LS")
            ctx.w(v, "LS")

    t = tap("gv", [1024, HID])
    if t is not None:
        ctx.w(sy, "LS", ctx.cnt["LS"])
        ctx.dma_s(t, gv_hbm)

    vf(v.tensor_scalar(out=gvmax[:], in0=gvmax[:], scalar1=1e-5,
                       scalar2=None, op0=ALU.max))
    vf(v.reciprocal(out=s2c[:], in_=gvmax[:]))
    vf(v.tensor_scalar(out=s2c[:], in0=s2c[:], scalar1=127.0, scalar2=None,
                       op0=ALU.mult))
    vf(v.tensor_scalar(out=is2c[:], in0=gvmax[:],
                       scalar1=1.0 / 127.0, scalar2=None, op0=ALU.mult))

    if _stop("gv"):
        return nc, taps

    bcast_row(bc2, mbf[:, 3 * C:4 * C], ones_row_bf)
    vf(v.tensor_scalar(out=col["tmp"][:, 0:1], in0=gam[:, 4:5],
                       scalar1=2.0, scalar2=None, op0=ALU.mult))
    dv_bc2 = ctx.inc(v.tensor_scalar(out=bc2[:], in0=bc2[:],
                                     scalar1=col["tmp"][:, 0:1],
                                     scalar2=None, op0=ALU.mult), "DV")

    for hq in range(4):
        dv_x2 = {}
        ctx.w(sy, "DV")
        for s in range(8):
            for hcl in range(4):
                i = s * 4 + hcl
                xqb = xq[i % 2]
                if i >= 2:
                    ctx.w(sy, "DV", dv_x2.get(i - 2, 0))
                tgt = ctx.dma_s(xqb[:],
                                gv_hbm[s * 128:(s + 1) * 128,
                                       hq * 2048 + hcl * 512:
                                       hq * 2048 + (hcl + 1) * 512])
                ctx.w(v, "LS", tgt)
                v.tensor_scalar(out=uf[:, 0:512], in0=xqb[:],
                                scalar1=s2c[:, s:s + 1], scalar2=128.0,
                                op0=ALU.mult, op1=ALU.add)
                v.tensor_copy(out=u32[:], in_=uf[:, 0:512])
                v.tensor_tensor(out=ef[:], in0=uf[:, 0:512], in1=u32[:],
                                op=ALU.subtract)
                v.scalar_tensor_tensor(out=ef[:], in0=ef[:],
                                       scalar=col["half"][:, 0:1],
                                       in1=u32[:], op0=ALU.is_ge,
                                       op1=ALU.add)
                dvx = ctx.inc(v.tensor_scalar(out=uf[:, 512:1024],
                                              in0=ef[:], scalar1=-128.0,
                                              scalar2=is2c[:, s:s + 1],
                                              op0=ALU.add, op1=ALU.mult),
                              "DV")
                ctx.w(te, "DV", dvx)
                for q4 in range(4):
                    mm_ = te.transpose(
                        psum[5][:, q4 * 128:(q4 + 1) * 128],
                        uf[:, 512 + q4 * 128:512 + (q4 + 1) * 128],
                        identf[:])
                pe_t = ctx.inc(mm_, "PE")
                ctx.w(v, "PE", pe_t)
                for q4 in range(4):
                    hid128 = hcl * 4 + q4
                    dst = bufA[hid128 // 2]
                    coff = (hid128 % 2) * 1024 + s * 128
                    last = v.tensor_copy(out=dst[:, coff:coff + 128],
                                         in_=psum[5][:, q4 * 128:
                                                     (q4 + 1) * 128])
                dv_x2[i] = ctx.inc(last, "DV")
                ctx.w(te, "DV", dv_x2[i])

        for oc in range(4):
            dv_wo = 0
            ctx.w(v, "PE")
            for hcl in range(16):
                src = owT[hq * 2048 + hcl * 128: hq * 2048 + (hcl + 1) * 128,
                          oc * 512:(oc + 1) * 512]
                wb = gstr[hcl % 2]
                ctx.w(g, "DV")
                lg_o = ctx.dma_g(wb[:, 0:512], src)
                ctx.w(v, "LG", lg_o)
                last = emit_ternary(wq_col[:, hcl * 512:(hcl + 1) * 512],
                                    wb[:, 0:512], 4, 5)
                dv_wo = ctx.inc(last, "DV")
            ctx.w(te, "DV", dv_wo)
            for tg in range(2):
                for s4 in range(4):
                    s = tg * 4 + s4
                    pb = psum[s4]
                    for hcl in range(16):
                        mm_ = te.matmul(pb[:], lhs_ := bufA[hcl // 2][
                            :, (hcl % 2) * 1024 + s * 128:
                            (hcl % 2) * 1024 + s * 128 + 128],
                            wq_col[:, hcl * 512:(hcl + 1) * 512],
                            start=(hcl == 0),
                            stop=(hq < 3 and hcl == 15))
                    if hq == 3:
                        mm_ = te.matmul(pb[:], ones_row_bf[:],
                                        brow_o[:, oc * 512:(oc + 1) * 512],
                                        start=False, stop=True)
                    pe_o = ctx.inc(mm_, "PE")
                    ctx.w(v, "PE", pe_o)
                    if s == 7 and oc == 0:
                        t0 = tap(f"ps7q{hq}", [128, 512])
                        if t0 is not None:
                            dtp = ctx.inc(v.tensor_copy(out=ef[:],
                                                        in_=pb[:]), "DV")
                            ctx.w(sy, "DV", dtp)
                            ctx.dma_s(t0, ef[:])
                    if hq == 0:
                        dv_a = ctx.inc(v.tensor_copy(
                            out=acc[s][:, oc * 512:(oc + 1) * 512],
                            in_=pb[:]), "DV")
                    else:
                        dv_a = ctx.inc(v.tensor_tensor(
                            out=acc[s][:, oc * 512:(oc + 1) * 512],
                            in0=acc[s][:, oc * 512:(oc + 1) * 512],
                            in1=pb[:], op=ALU.add), "DV")
                    ctx.w(te, "DV", dv_a)

    t = tap("acc0", [128, 2048])
    if t is not None:
        ctx.w(sy, "DV")
        ctx.dma_s(t, acc[0][:])
    t = tap("acc7", [128, 2048])
    if t is not None:
        ctx.w(sy, "DV")
        ctx.dma_s(t, acc[7][:])
    t = tap("x2fm0", [128, 2048], BF16)
    if t is not None:
        ctx.w(sy, "DV")
        ctx.dma_s(t, bufA[0][:])
    dv_y = 0
    ctx.w(sy, "DV")
    for s in range(8):
        if s >= 1:
            ctx.w(sy, "DV", dv_y)
        tgt = ctx.dma_s(xo_t[:], xnew_hbm[s * 128:(s + 1) * 128, :])
        ctx.w(v, "LS", tgt)
        ctx.w(v, "DV", dv_bc2)
        v.tensor_tensor(out=uf[:], in0=acc[s][:], in1=bc2[:], op=ALU.mult)
        dv_y = ctx.inc(v.tensor_tensor(out=uf[:], in0=uf[:], in1=xo_t[:],
                                       op=ALU.add), "DV")
        ctx.w(sy, "DV", dv_y)
        ctx.dma_s(y_out[s * 128:(s + 1) * 128, :], uf[:])
        ctx.w(v, "LS")

    sy.wait_ge(ctx.sem["LS"], ctx.cnt["LS"])
    return nc, taps


def _host_prep(L0_attn_x, L0_attn_W_qkv, L0_attn_W_o, L0_ffn_gate_w,
               L0_ffn_gate_b, L0_ffn_val_w, L0_ffn_val_b, L0_ffn_out_w,
               L0_ffn_out_b, L0_c, L0_wmod):
    import ml_dtypes
    x = np.ascontiguousarray(np.asarray(L0_attn_x, dtype=np.float32))
    wqkv = np.asarray(L0_attn_W_qkv, dtype=np.float32)
    perm = np.empty(128, dtype=np.int64)
    perm[0:64] = np.arange(64) * 2
    perm[64:128] = np.arange(64) * 2 + 1
    rows = np.arange(3 * C)
    sec, rem = rows // C, rows % C
    h_, d_ = rem // D, rem % D
    newsrc = np.where(sec < 2, sec * C + h_ * D + perm[d_], rows)
    wqkvT = np.ascontiguousarray(wqkv[newsrc].T)
    woT = np.ascontiguousarray(np.asarray(L0_attn_W_o, np.float32).T)
    gwT = np.ascontiguousarray(np.asarray(L0_ffn_gate_w, np.float32).T)
    vwT = np.ascontiguousarray(np.asarray(L0_ffn_val_w, np.float32).T)
    owT = np.ascontiguousarray(np.asarray(L0_ffn_out_w, np.float32).T)
    wmod = np.ascontiguousarray(np.asarray(L0_wmod, np.float32))
    carr = np.asarray(L0_c, dtype=np.float32)
    gb = np.ascontiguousarray(np.asarray(L0_ffn_gate_b,
                                         np.float32).reshape(1, HID))
    vb = np.ascontiguousarray(np.asarray(L0_ffn_val_b,
                                         np.float32).reshape(1, HID))
    ob = np.ascontiguousarray(np.asarray(L0_ffn_out_b,
                                         np.float32).reshape(1, C))
    identb = np.eye(128, dtype=ml_dtypes.bfloat16)
    identf = np.eye(128, dtype=np.float32)

    in_maps = []
    for core in range(8):
        b, half = core // 2, core % 2
        if half == 1:
            xb_eff = x[b]
        else:
            xb_eff = np.concatenate([x[b, 1024:2048], x[b, 0:1024]], axis=0)
        bt = np.full((128, NCOMB), EXP_SHIFT, dtype=np.float32)
        if half == 0:
            for j, (qb, c_) in enumerate(COMBOS):
                if c_ < 8:
                    bt[:, j] = DEAD
        pos = np.zeros((128, 16), dtype=np.float32)
        for s in range(16):
            if half == 0:
                base = (s - 8) * 128 if s >= 8 else s * 128
            else:
                base = s * 128
            pos[:, s] = base + np.arange(128)
        ccol_ = np.ascontiguousarray(carr[b].reshape(16, 128).T)
        in_maps.append({
            "xb": np.ascontiguousarray(xb_eff), "wqkvT": wqkvT, "woT": woT,
            "gwT": gwT, "vwT": vwT, "owT": owT, "wmod": wmod, "ccol": ccol_,
            "gb_in": gb, "vb_in": vb, "ob_in": ob, "biastab": bt,
            "pos_col": pos, "identb": identb, "identf": identf,
        })
    return in_maps


def kernel(**inputs):
    in_maps = _host_prep(**inputs)
    nc, _ = build()
    res = run_bass_kernel_spmd(nc, in_maps, core_ids=list(range(8)))
    out = np.empty((B, T, C), dtype=np.float32)
    for core in range(8):
        b, half = core // 2, core % 2
        out[b, half * 1024:(half + 1) * 1024] = res.results[core]["y"]
    return out


# version 03
# speedup vs baseline: 1.1171x; 1.1171x over previous
import numpy as np
import concourse.bass as bass
import concourse.mybir as mybir
from concourse.bass_utils import run_bass_kernel_spmd

F32 = mybir.dt.float32
BF16 = mybir.dt.bfloat16
I32 = mybir.dt.int32
ALU = mybir.AluOpType
AF = mybir.ActivationFunctionType
AX = mybir.AxisListType

B, T, C, H, D = 4, 2048, 2048, 16, 128
HID = 4 * C
NT = 16
LN_EPS = 1e-5
RMS_EPS = 1.1920929e-07
EXP_SHIFT = -11.5
DEAD = -1e9
SCALE = float(1.0 / np.sqrt(np.float32(128.0)))
TWO_PI = float(2 * np.pi)
KB = 1024

COMBOS = [(qb, c) for qb in (0, 1) for c in range(12 if qb == 0 else 16)]
NCOMB = len(COMBOS)
CIDX = {qc: i for i, qc in enumerate(COMBOS)}


class Ctx:

    def __init__(self, nc):
        self.nc = nc
        self.sem = {}
        self.cnt = {}
        for name in ("LS", "LG", "PE", "DV", "AC", "GP"):
            self.sem[name] = nc.alloc_semaphore(name)
            self.cnt[name] = 0

    def inc(self, inst, s, n=1):
        inst.then_inc(self.sem[s], n)
        self.cnt[s] += n
        return self.cnt[s]

    def w(self, eng, s, val=None):
        if val is None:
            val = self.cnt[s]
        eng.wait_ge(self.sem[s], val)

    def dma_s(self, out, in_):
        return self.inc(self.nc.sync.dma_start(out=out, in_=in_), "LS", 16)

    def dma_g(self, out, in_):
        return self.inc(self.nc.gpsimd.dma_start(out=out, in_=in_), "LG", 16)


def build(debug_taps=(), stop_phase=None):
    nc = bass.Bass("TRN2", target_bir_lowering=False, debug=False,
                   detect_race_conditions=False)
    ctx = Ctx(nc)
    v, sc, te, g, sy = nc.vector, nc.scalar, nc.tensor, nc.gpsimd, nc.sync

    xb = nc.dram_tensor("xb", [T, C], F32, kind="ExternalInput").ap()
    wqkvT = nc.dram_tensor("wqkvT", [C, 3 * C], F32, kind="ExternalInput").ap()
    woT = nc.dram_tensor("woT", [C, C], F32, kind="ExternalInput").ap()
    gwT = nc.dram_tensor("gwT", [C, HID], F32, kind="ExternalInput").ap()
    vwT = nc.dram_tensor("vwT", [C, HID], F32, kind="ExternalInput").ap()
    owT = nc.dram_tensor("owT", [HID, C], F32, kind="ExternalInput").ap()
    wmod = nc.dram_tensor("wmod", [C, 6 * C], F32, kind="ExternalInput").ap()
    ccol = nc.dram_tensor("ccol", [128, 16], F32, kind="ExternalInput").ap()
    gb_in = nc.dram_tensor("gb_in", [1, HID], F32, kind="ExternalInput").ap()
    vb_in = nc.dram_tensor("vb_in", [1, HID], F32, kind="ExternalInput").ap()
    ob_in = nc.dram_tensor("ob_in", [1, C], F32, kind="ExternalInput").ap()
    biastab = nc.dram_tensor("biastab", [128, NCOMB], F32,
                             kind="ExternalInput").ap()
    pos_in = nc.dram_tensor("pos_col", [128, 16], F32,
                            kind="ExternalInput").ap()
    identb_in = nc.dram_tensor("identb", [128, 128], BF16,
                               kind="ExternalInput").ap()
    identf_in = nc.dram_tensor("identf", [128, 128], F32,
                               kind="ExternalInput").ap()

    y_out = nc.dram_tensor("y", [1024, C], F32, kind="ExternalOutput").ap()
    gv_hbm = nc.dram_tensor("gv_hbm", [1024, HID], F32, kind="Internal").ap()
    xnew_hbm = nc.dram_tensor("xnew_hbm", [1024, C], F32, kind="Internal").ap()

    taps = {}

    def tap(name, shape, dtype=F32):
        if name in debug_taps:
            taps[name] = nc.dram_tensor("tap_" + name, shape, dtype,
                                        kind="ExternalOutput").ap()
        return taps.get(name)

    ARENA_SZ = 204 * KB
    arena = nc.alloc_sbuf_tensor("arena", [128, ARENA_SZ], mybir.dt.uint8)
    A0 = nc.lookup_mloc(arena).addr

    _static_iv = []

    def place(name, shape, dtype, off):
        h_ = nc.alloc_sbuf_tensor_at(name, shape, dtype, offset=A0 + off)
        nbytes = int(np.prod(shape[1:])) * mybir.dt.size(dtype)
        assert off + nbytes <= ARENA_SZ, (name, off, nbytes)
        if off + nbytes <= 68 * KB:
            for o0, o1, n0 in _static_iv:
                assert off >= o1 or off + nbytes <= o0, (name, n0, off)
            _static_iv.append((off, off + nbytes, name))
        return h_

    uf = place("uf", [128, 2048], F32, 0)
    u32 = place("u32", [128, 512], I32, 8 * KB)
    ef = place("ef", [128, 512], F32, 10 * KB)
    trash_bf = place("trash_bf", [128, 2048], BF16, 12 * KB)
    mbf = place("mbf", [1, 4 * C], BF16, 16 * KB)
    sin_tm = place("sin_tm", [128, 1024], BF16, 32 * KB)
    cos_tm = place("cos_tm", [128, 1024], BF16, 34 * KB)
    gstr = [place(f"gstr{i}", [128, 2048], BF16, 36 * KB + i * 4 * KB)
            for i in range(2)]
    brow_o = place("brow_o", [1, C], BF16, 44 * KB)
    exp_sb = [place(f"exp{i}", [128, 512], BF16, 48 * KB + i * KB)
              for i in range(2)]
    inv_sb = place("inv_sb", [1, 512], F32, 50 * KB)
    inv_b = place("inv_b", [128, 512], BF16, 52 * KB)
    gate_s = place("gate_s", [128, 512], F32, 53 * KB)
    gv_s = place("gv_s", [128, 512], F32, 55 * KB)
    xq = [place(f"xq{i}", [128, 512], F32, 57 * KB + i * 2 * KB)
          for i in range(2)]
    wq_a = place("wq_a", [128, 512], BF16, 61 * KB)
    wq_b = place("wq_b", [128, 512], BF16, 62 * KB)
    ident = place("ident_sb", [128, 128], BF16, 63 * KB)
    identf = place("identf_sb", [128, 128], F32, 63 * KB + 256)
    tri = place("tri", [128, 128], BF16, 63 * KB + 768)
    btab = place("btab", [128, NCOMB], F32, 64 * KB)
    pcol = place("pcol", [128, 16], F32, 64 * KB + 128)
    cc_sb = place("cc_sb", [128, 16], F32, 64 * KB + 192)
    silu_c = place("silu_c", [128, 16], BF16, 64 * KB + 256)
    silu_cf = nc.alloc_sbuf_tensor_at("silu_cf", [128, 16], F32,
                                      offset=A0 + 8 * KB + 1024)
    invfq = place("invfq", [1, 64], F32, 64 * KB + 288)
    ivb = place("ivb", [128, 64], F32, 64 * KB + 544)
    gabs = place("gabs", [128, 192], F32, 64 * KB + 800)
    grow = place("grow", [1, 16], F32, 64 * KB + 1568)
    gam = place("gam", [128, 8], F32, 64 * KB + 1632)
    gvmax = place("gvmax", [128, 8], F32, 64 * KB + 1664)
    s2c = place("s2c", [128, 8], F32, 64 * KB + 1696)
    is2c = place("is2c", [128, 8], F32, 64 * KB + 1728)
    rms_c = place("rms_c", [128, 12], F32, 64 * KB + 1760)
    ones_col_bf = place("ones_col_bf", [128, 1], BF16, 64 * KB + 1824)
    ones_col_f = place("ones_col_f", [128, 1], F32, 64 * KB + 1856)
    ones_row_bf = place("ones_row_bf", [1, 128], BF16, 64 * KB + 1888)
    ones_row_f = place("ones_row_f", [1, 128], F32, 64 * KB + 2144)
    col = {}
    for i, name in enumerate(("sum", "mean", "var", "invs", "half", "sc",
                              "c2pi", "gmax", "srow", "invsr", "tmp")):
        col[name] = place("col_" + name, [128, 2], F32,
                          64 * KB + 2688 + i * 32)
    dm3 = place("dm3", [128, 512], BF16, 64 * KB + 3040)

    B0 = 68 * KB
    mrow = place("mrow", [1, 6 * C], F32, B0)
    bufA = [place(f"bufA{i}", [128, 2048], BF16, B0 + i * 4 * KB)
            for i in range(16)]
    attn_fm = [place(f"attnfm{i}", [128, 1024], BF16,
                     B0 + 64 * KB + i * 2 * KB) for i in range(16)]
    x_t = [place(f"x_t{i}", [128, 2048], F32, B0 + 64 * KB + i * 8 * KB)
           for i in range(2)]
    bcA1 = place("bcA1", [128, 2048], F32, B0 + 96 * KB)
    bcB1 = place("bcB1", [128, 2048], F32, B0 + 104 * KB)
    wslice = place("wslice", [128, 4096], BF16, B0 + 96 * KB)
    q_tm = place("q_tm", [128, 2048], BF16, B0 + 104 * KB)
    k_tm = place("k_tm", [128, 4096], BF16, B0 + 108 * KB)
    v_tm = place("v_tm", [128, 4096], BF16, B0 + 116 * KB)
    q_fm = [place(f"qfm{i}", [128, 1024], BF16, B0 + 124 * KB + i * 2 * KB)
            for i in range(2)]
    k_fm = [place(f"kfm{i}", [128, 2048], BF16, B0 + 128 * KB + i * 4 * KB)
            for i in range(2)]
    xnew_t = [place(f"xnew{i}", [128, 2048], F32, B0 + 96 * KB + i * 8 * KB)
              for i in range(2)]
    xo_t = place("xo_t", [128, 2048], F32, B0 + 112 * KB)
    bc2 = place("bc2", [128, 2048], F32, B0 + 120 * KB)
    bcA2 = place("bcA2", [128, 2048], F32, B0 + 120 * KB)
    bcB2 = place("bcB2", [128, 2048], F32, B0 + 128 * KB)
    gw_bf = place("gw_bf", [128, 8192], BF16, B0 + 96 * KB)
    vw_bf = place("vw_bf", [128, 8192], BF16, B0 + 112 * KB)
    wg_op = place("wg_op", [128, 8192], BF16, B0 + 0)
    wv_op = place("wv_op", [128, 8192], BF16, B0 + 16 * KB)
    brow_g = place("brow_g", [1, HID], BF16, B0 + 32 * KB)
    brow_v = place("brow_v", [1, HID], BF16, B0 + 48 * KB)
    acc = [place(f"acc{i}", [128, 2048], F32, B0 + 32 * KB + i * 8 * KB)
           for i in range(8)]
    wq_col = place("wq_col", [128, 8192], BF16, B0 + 96 * KB)

    psum = [nc.alloc_psum_tensor(f"pb{i}", [128, 512], F32) for i in range(6)]
    prow = nc.alloc_psum_tensor("prow", [1, 512], F32)
    ptr = nc.alloc_psum_tensor("ptr", [128, 1024], BF16)

    h1_fm = bufA
    wo_sb = bufA
    q_h2 = attn_fm

    def vf(inst):
        ctx.w(v, "DV", ctx.inc(inst, "DV"))

    def _stop(ph):
        if stop_phase == ph:
            sy.wait_ge(ctx.sem["LS"], ctx.cnt["LS"])
            return True
        return False


    ctx.dma_s(ident[:], identb_in)
    ctx.dma_s(identf[:], identf_in)
    ctx.dma_s(btab[:], biastab)
    ctx.dma_s(pcol[:], pos_in)
    ld0 = ctx.dma_s(cc_sb[:], ccol)
    ctx.dma_g(brow_o[:], ob_in)

    v.memset(ones_col_bf[:], 1.0)
    v.memset(ones_col_f[:], 1.0)
    v.memset(ones_row_bf[:], 1.0)
    v.memset(ones_row_f[:], 1.0)
    v.memset(col["half"][:], 0.5)
    v.memset(col["sc"][:], SCALE)
    v.memset(col["c2pi"][:], TWO_PI)
    dv_ms = ctx.inc(v.memset(trash_bf[:], 0.0), "DV")

    g.memset(tri[:], 1.0)
    gp_t = ctx.inc(g.engine_nop(), "GP")
    g.wait_ge(ctx.sem["GP"], gp_t)
    g.affine_select(tri[:], tri[:], pattern=[[1, 128]],
                    compare_op=ALU.is_ge, fill=0.0, base=0,
                    channel_multiplier=-1)
    g.iota(invfq[:], pattern=[[1, 64]], base=0, channel_multiplier=0,
           allow_small_or_imprecise_dtypes=True)
    gp0 = ctx.inc(g.engine_nop(), "GP")

    ctx.w(v, "GP", gp0)
    vf(v.memset(dm3[:], 0.0))
    vf(v.tensor_copy(out=dm3[:, 384:512], in_=tri[:]))
    v.memset(col["tmp"][:], float(-np.log(10000.0) / 64.0))
    dv0 = ctx.inc(v.engine_nop(), "DV")
    ctx.w(sc, "GP", gp0)
    ctx.w(sc, "DV", dv0)
    ac0 = ctx.inc(sc.activation(out=invfq[:], in_=invfq[:], func=AF.Exp,
                                scale=col["tmp"][0:1, 0:1]), "AC")
    ctx.w(te, "AC", ac0)
    pe0 = ctx.inc(te.matmul(psum[0][:, 0:64], ones_row_f[:], invfq[:],
                            start=True, stop=True), "PE")
    ctx.w(v, "PE", pe0)
    ctx.w(v, "LS", ld0)
    dv1 = ctx.inc(v.tensor_copy(out=ivb[:], in_=psum[0][:, 0:64]), "DV")

    def emit_sincos(dst, s, phase_add):
        ctx.w(v, "AC")
        v.tensor_scalar(out=uf[:, 64:128], in0=ivb[:],
                        scalar1=pcol[:, s:s + 1], scalar2=None, op0=ALU.mult)
        v.tensor_scalar(out=uf[:, 0:64], in0=uf[:, 64:128],
                        scalar1=float(1.0 / TWO_PI), scalar2=phase_add,
                        op0=ALU.mult, op1=ALU.add)
        v.tensor_copy(out=u32[:, 0:64], in_=uf[:, 0:64])
        v.tensor_tensor(out=ef[:, 0:64], in0=uf[:, 0:64], in1=u32[:, 0:64],
                        op=ALU.subtract)
        v.scalar_tensor_tensor(out=ef[:, 0:64], in0=ef[:, 0:64],
                               scalar=col["half"][:, 0:1], in1=u32[:, 0:64],
                               op0=ALU.is_ge, op1=ALU.add)
        dvx = ctx.inc(v.tensor_tensor(out=uf[:, 0:64], in0=uf[:, 0:64],
                                      in1=ef[:, 0:64], op=ALU.subtract),
                      "DV")
        ctx.w(sc, "DV", dvx)
        return ctx.inc(sc.activation(out=dst, in_=uf[:, 0:64], func=AF.Sin,
                                     scale=col["c2pi"][:, 0:1]), "AC")

    ctx.w(v, "DV", dv1)
    ac_tab = 0
    for s in range(16):
        emit_sincos(sin_tm[:, s * 64:(s + 1) * 64], s, 0.0)
        ac_tab = emit_sincos(cos_tm[:, s * 64:(s + 1) * 64], s, 0.25)

    t = tap("sin", [128, 1024], BF16)
    if t is not None:
        ctx.w(sy, "AC", ac_tab)
        ctx.dma_s(t, sin_tm[:])
    t = tap("cos", [128, 1024], BF16)
    if t is not None:
        ctx.w(sy, "AC", ac_tab)
        ctx.dma_s(t, cos_tm[:])

    if _stop("tables"):
        return nc, taps

    ctx.w(sc, "LS", ld0)
    sc.activation(out=silu_c[:], in_=cc_sb[:], func=AF.Silu)
    ac_sc = ctx.inc(sc.activation(out=silu_cf[:], in_=cc_sb[:],
                                  func=AF.Silu), "AC")
    ctx.w(te, "AC", ac_sc)
    ctx.w(te, "DV")
    pe_tile = {}
    dv_mod = 0
    ring_bf = [gstr[0], gstr[1], None, None]
    ring_f = [x_t[0], x_t[1], bcA1, bcB1]

    def _wmod_buf(i, gate_jb):
        if gate_jb:
            return ring_f[i % 4]
        if i % 4 < 2:
            return gstr[i % 4]
        return None

    wm_dma = {}

    def _emit_wmod_dma(i):
        jb, ck = i // 16, i % 16
        gate_jb = jb in (2, 5)
        wb = _wmod_buf(i, gate_jb)
        wb_ap = (wb[:] if wb is not None
                 else wslice[:, (i % 4 - 2) * 2048:(i % 4 - 1) * 2048])
        if i >= 4:
            ctx.w(g, "PE", pe_tile[i - 4])
        wm_dma[i] = (ctx.dma_g(wb_ap, wmod[ck * 128:(ck + 1) * 128,
                                           jb * 2048:(jb + 1) * 2048]),
                     wb_ap)

    for i in range(4):
        _emit_wmod_dma(i)
    for jb in range(6):
        gate_jb = jb in (2, 5)
        for ck in range(16):
            i = jb * 16 + ck
            lhs_c = silu_cf if gate_jb else silu_c
            tgt, wb_ap = wm_dma[i]
            ctx.w(te, "LG", tgt)
            for q4 in range(4):
                mm_ = te.matmul(psum[q4][0:1, :], lhs_c[:, ck:ck + 1],
                                wb_ap[:, q4 * 512:(q4 + 1) * 512],
                                start=(ck == 0), stop=(ck == 15))
            pe_tile[i] = ctx.inc(mm_, "PE")
            if i + 4 < 96:
                _emit_wmod_dma(i + 4)
        ctx.w(v, "PE", pe_tile[jb * 16 + 15])
        for q4 in range(4):
            off = jb * 2048 + q4 * 512
            dv_mod = ctx.inc(v.tensor_copy(out=mrow[:, off:off + 512],
                                           in_=psum[q4][0:1, :]), "DV")
        ctx.w(te, "DV", dv_mod)

    SA, CA, GA, SB, CB, GB = (i * C for i in range(6))
    v.tensor_scalar(out=mrow[:, CA:CA + C], in0=mrow[:, CA:CA + C],
                    scalar1=1.0, scalar2=None, op0=ALU.add)
    v.tensor_scalar(out=mrow[:, CB:CB + C], in0=mrow[:, CB:CB + C],
                    scalar1=1.0, scalar2=None, op0=ALU.add)
    for off in (GA, GB):
        sl = mrow[:, off:off + C]
        v.tensor_scalar(out=uf[0:1, 0:C], in0=sl, scalar1=-1.0,
                        scalar2=0.25, op0=ALU.mult, op1=ALU.max)
        v.scalar_tensor_tensor(out=uf[0:1, 0:C], in0=sl, scalar=0.25,
                               in1=uf[0:1, 0:C], op0=ALU.max, op1=ALU.max)
        v.tensor_scalar(out=x_t[0][0:1, 0:C], in0=sl, scalar1=0.0,
                        scalar2=None, op0=ALU.is_ge)
        v.tensor_scalar(out=x_t[0][0:1, 0:C], in0=x_t[0][0:1, 0:C],
                        scalar1=2.0, scalar2=-1.0, op0=ALU.mult, op1=ALU.add)
        v.tensor_tensor(out=sl, in0=uf[0:1, 0:C], in1=x_t[0][0:1, 0:C],
                        op=ALU.mult)
    v.tensor_copy(out=mbf[:, 0:C], in_=mrow[:, CB:CB + C])
    v.tensor_copy(out=mbf[:, C:2 * C], in_=mrow[:, SB:SB + C])
    v.tensor_copy(out=mbf[:, 2 * C:3 * C], in_=mrow[:, GA:GA + C])
    dv_m = ctx.inc(v.tensor_copy(out=mbf[:, 3 * C:4 * C],
                                 in_=mrow[:, GB:GB + C]), "DV")
    t = tap("mod", [1, 6 * C])
    if t is not None:
        ctx.w(sy, "DV", dv_m)
        ctx.dma_s(t, mrow[:])

    def bcast_row(dst, src_row, ones):
        for ocn in range(4):
            ctx.w(te, "DV")
            mm_ = te.matmul(psum[4 + ocn % 2][:], ones[:],
                            src_row[:, ocn * 512:(ocn + 1) * 512],
                            start=True, stop=True)
            pe_b = ctx.inc(mm_, "PE")
            ctx.w(v, "PE", pe_b)
            ctx.inc(v.tensor_copy(out=dst[:, ocn * 512:(ocn + 1) * 512],
                                  in_=psum[4 + ocn % 2][:]), "DV")

    t = tap("mbft", [1, 4 * C], BF16)
    if t is not None:
        ctx.w(sy, "DV", dv_m)
        ctx.dma_s(t, mbf[:])
    ctx.w(te, "DV", dv_m)
    bcast_row(bcA1, mrow[:, CA:CA + C], ones_row_f)
    bcast_row(bcB1, mrow[:, SA:SA + C], ones_row_f)

    if _stop("mod"):
        return nc, taps

    def emit_ln_apply(src_tile, scale_b, shift_b, out_bf):
        vf(v.reduce_sum(col["sum"][:, 0:1], src_tile, axis=AX.X))
        vf(v.tensor_scalar(out=col["mean"][:, 0:1], in0=col["sum"][:, 0:1],
                           scalar1=1.0 / 2048.0, scalar2=None, op0=ALU.mult))
        vf(v.scalar_tensor_tensor(out=trash_bf[:], in0=src_tile,
                                  scalar=col["mean"][:, 0:1], in1=src_tile,
                                  op0=ALU.subtract, op1=ALU.mult,
                                  accum_out=col["var"][:, 0:1]))
        vf(v.tensor_scalar(out=col["var"][:, 0:1], in0=col["var"][:, 0:1],
                           scalar1=1.0 / 2048.0, scalar2=LN_EPS,
                           op0=ALU.mult, op1=ALU.add))
        dvs = ctx.inc(v.reciprocal(out=col["var"][:, 1:2],
                                   in_=col["var"][:, 0:1]), "DV")
        ctx.w(sc, "DV", dvs)
        acs = ctx.inc(sc.activation(out=col["invs"][:, 0:1],
                                    in_=col["var"][:, 1:2], func=AF.Sqrt),
                      "AC")
        ctx.w(v, "AC", acs)
        v.tensor_scalar(out=uf[:], in0=src_tile, scalar1=col["mean"][:, 0:1],
                        scalar2=col["invs"][:, 0:1], op0=ALU.subtract,
                        op1=ALU.mult)
        v.tensor_tensor(out=uf[:], in0=uf[:], in1=scale_b[:], op=ALU.mult)
        return ctx.inc(v.tensor_tensor(out=out_bf, in0=uf[:], in1=shift_b[:],
                                       op=ALU.add), "DV")

    def emit_transpose16(src_bf, dst_fn, dst_col):
        last_dv = 0
        for cg in range(2):
            ctx.w(te, "DV")
            for c8 in range(8):
                cch = cg * 8 + c8
                mm_ = te.transpose(ptr[:, c8 * 128:(c8 + 1) * 128],
                                   src_bf[:, cch * 128:(cch + 1) * 128],
                                   ident[:])
            pe_t = ctx.inc(mm_, "PE")
            ctx.w(v, "PE", pe_t)
            for c8 in range(8):
                cch = cg * 8 + c8
                last = v.tensor_copy(
                    out=dst_fn(cch)[:, dst_col:dst_col + 128],
                    in_=ptr[:, c8 * 128:(c8 + 1) * 128])
            last_dv = ctx.inc(last, "DV")
        return last_dv

    dv_h1 = {}
    ctx.w(sy, "PE")
    ctx.w(sy, "DV")
    for s in range(NT):
        xt = x_t[s % 2]
        if s >= 2:
            ctx.w(sy, "DV", dv_h1[s - 2])
        tgt = ctx.dma_s(xt[:], xb[s * 128:(s + 1) * 128, :])
        ctx.w(v, "LS", tgt)
        dvln = emit_ln_apply(xt[:], bcA1, bcB1, trash_bf[:])
        if s == 0:
            t0 = tap("h1tm", [128, 2048], BF16)
            if t0 is not None:
                ctx.w(sy, "DV", dvln)
                ctx.dma_s(t0, trash_bf[:])
            t0 = tap("x0", [128, 2048])
            if t0 is not None:
                ctx.w(sy, "DV", dvln)
                ctx.dma_s(t0, xt[:])
        dv_h1[s] = emit_transpose16(trash_bf, lambda cch: h1_fm[cch],
                                    s * 128)

    t = tap("h1fm", [128, 2048 * 16], BF16)
    if t is not None:
        ctx.w(sy, "DV", dv_h1[NT - 1])
        for cch in range(16):
            ctx.dma_s(t[:, cch * 2048:(cch + 1) * 2048], h1_fm[cch][:])

    if _stop("h1"):
        return nc, taps

    gamma_pieces = []
    for i in range(16):
        for p in range(4):
            gamma_pieces.append((gwT[i * 128:(i + 1) * 128,
                                     p * 2048:(p + 1) * 2048], i * 4 + p))
    for i in range(16):
        for p in range(4):
            gamma_pieces.append((vwT[i * 128:(i + 1) * 128,
                                     p * 2048:(p + 1) * 2048],
                                 64 + i * 4 + p))
    for i in range(64):
        gamma_pieces.append((owT[i * 128:(i + 1) * 128, :], 128 + i))
    gamma_ac = {}
    gamma_k = [0]

    def emit_gamma(n):
        for _ in range(n):
            k = gamma_k[0]
            if k >= len(gamma_pieces):
                return
            gamma_k[0] = k + 1
            src, gcol = gamma_pieces[k]
            wb = gstr[k % 2]
            if k >= 2:
                ctx.w(g, "AC", gamma_ac[k - 2])
            else:
                ctx.w(g, "PE")
            tgt = ctx.dma_g(wb[:], src)
            ctx.w(sc, "LG", tgt)
            gamma_ac[k] = ctx.inc(
                sc.activation(out=trash_bf[:], in_=wb[:], func=AF.Abs,
                              accum_out=gabs[:, gcol:gcol + 1]), "AC")

    SECQ, SECK, SECV = 0, 2048, 4096

    proj_bank = [0, 0]
    for p2 in range(8):
        heads = [p2 * 2, p2 * 2 + 1]
        for sec_i, (sec_off, tiles) in enumerate(
                ((SECQ, range(8, 16)), (SECK, range(16)), (SECV, range(16)))):
            csel = sec_off + p2 * 256
            w3 = wqkvT[:, csel:csel + 256].rearrange("(k p) n -> p k n",
                                                     p=128)
            ctx.w(g, "PE")
            ctx.w(g, "DV")
            tgt = ctx.dma_g(wslice[:], w3)
            ctx.w(te, "LG", tgt)
            for si, s in enumerate(tiles):
                pb = psum[si % 2]
                ctx.w(te, "DV", proj_bank[si % 2])
                for cch in range(16):
                    mm_ = te.matmul(pb[:, 0:256],
                                    h1_fm[cch][:, s * 128:(s + 1) * 128],
                                    wslice[:, cch * 256:(cch + 1) * 256],
                                    start=(cch == 0), stop=(cch == 15))
                pe_mm = ctx.inc(mm_, "PE")
                ctx.w(v, "PE", pe_mm)
                if sec_i == 0:
                    dst_ap = q_tm[:, (s - 8) * 256:(s - 7) * 256]
                elif sec_i == 1:
                    dst_ap = k_tm[:, s * 256:(s + 1) * 256]
                else:
                    dst_ap = v_tm[:, s * 256:(s + 1) * 256]
                proj_bank[si % 2] = ctx.inc(
                    v.tensor_copy(out=dst_ap, in_=pb[:, 0:256]), "DV")

        def rope_rms(buf, s, tile_idx):
            base = tile_idx * 256
            so = s * 64
            sin_s = sin_tm[:, so:so + 64]
            cos_s = cos_tm[:, so:so + 64]
            for hl in range(2):
                lo = buf[:, base + hl * 128: base + hl * 128 + 64]
                hi = buf[:, base + hl * 128 + 64: base + hl * 128 + 128]
                v.tensor_tensor(out=uf[:, 0:64], in0=lo, in1=cos_s,
                                op=ALU.mult)
                v.tensor_tensor(out=uf[:, 64:128], in0=hi, in1=sin_s,
                                op=ALU.mult)
                v.tensor_tensor(out=uf[:, 128:192], in0=hi, in1=cos_s,
                                op=ALU.mult)
                v.tensor_tensor(out=uf[:, 192:256], in0=lo, in1=sin_s,
                                op=ALU.mult)
                v.tensor_tensor(out=lo, in0=uf[:, 0:64], in1=uf[:, 64:128],
                                op=ALU.subtract)
                v.tensor_tensor(out=hi, in0=uf[:, 128:192],
                                in1=uf[:, 192:256], op=ALU.add)
                full = buf[:, base + hl * 128: base + hl * 128 + 128]
                ii = v.scalar_tensor_tensor(
                    out=trash_bf[:, 0:128], in0=full,
                    scalar=1.0, in1=full, op0=ALU.mult, op1=ALU.mult,
                    accum_out=rms_c[:, hl:hl + 1])
            vf(ii)
            vf(v.tensor_scalar(out=rms_c[:, 0:2], in0=rms_c[:, 0:2],
                               scalar1=1.0 / 128.0, scalar2=RMS_EPS,
                               op0=ALU.mult, op1=ALU.add))
            dvs = ctx.inc(v.reciprocal(out=rms_c[:, 2:4],
                                       in_=rms_c[:, 0:2]), "DV")
            ctx.w(sc, "DV", dvs)
            acs = ctx.inc(sc.activation(out=rms_c[:, 4:6], in_=rms_c[:, 2:4],
                                        func=AF.Sqrt), "AC")
            ctx.w(v, "AC", acs)
            for hl in range(2):
                sl = buf[:, base + hl * 128: base + hl * 128 + 128]
                v.tensor_scalar(out=sl, in0=sl,
                                scalar1=rms_c[:, 4 + hl:5 + hl],
                                scalar2=None, op0=ALU.mult)
            return ctx.inc(v.engine_nop(), "DV")

        if p2 == 0:
            t0 = tap("qkv0", [128, 2048 * 3], BF16)
            if t0 is not None:
                ctx.w(sy, "DV")
                ctx.dma_s(t0[:, 0:2048], q_tm[:])
                ctx.dma_s(t0[:, 2048:2048 + 4096], k_tm[:])
                ctx.dma_s(t0[:, 6144 - 2048:6144], v_tm[:, 0:2048])

        for s in range(8, 16):
            rope_rms(q_tm, s, s - 8)
        dv_rope = 0
        for s in range(16):
            dv_rope = rope_rms(k_tm, s, s)

        if p2 == 0:
            t0 = tap("rope0", [128, 2048 * 3], BF16)
            if t0 is not None:
                ctx.w(sy, "DV")
                ctx.dma_s(t0[:, 0:2048], q_tm[:])
                ctx.dma_s(t0[:, 2048:6144], k_tm[:])

        for hl in range(2):
            hglob = heads[hl]
            jobs = [(q_tm, s * 256 + hl * 128, q_fm[hl], s * 128)
                    for s in range(8)]
            jobs += [(k_tm, s * 256 + hl * 128, k_fm[hl], s * 128)
                     for s in range(16)]
            ctx.w(te, "DV", dv_rope)
            for j0 in range(0, len(jobs), 8):
                batch = jobs[j0:j0 + 8]
                ctx.w(te, "DV")
                for bi, (sb_, soff, dbuf, doff) in enumerate(batch):
                    mm_ = te.transpose(ptr[:, bi * 128:(bi + 1) * 128],
                                       sb_[:, soff:soff + 128], ident[:])
                pe_t = ctx.inc(mm_, "PE")
                ctx.w(v, "PE", pe_t)
                for bi, (sb_, soff, dbuf, doff) in enumerate(batch):
                    last = v.tensor_copy(out=dbuf[:, doff:doff + 128],
                                         in_=ptr[:, bi * 128:(bi + 1) * 128])
                ctx.inc(last, "DV")
            dv_tp = ctx.cnt["DV"]
            ctx.w(te, "DV", dv_tp)

            for qb in range(2):
                nck = 12 if qb == 0 else 16
                ac_exp = {}
                pe_use = {}
                pe_sc = {}
                ctx.w(sc, "PE")
                ctx.w(te, "AC")

                def emit_scores(c_, qb=qb, hl=hl):
                    ps = psum[2 + c_ % 2]
                    if c_ >= 2:
                        ctx.w(te, "AC", ac_exp[c_ - 2])
                    mm_ = te.matmul(ps[:],
                                    k_fm[hl][:, c_ * 128:(c_ + 1) * 128],
                                    q_fm[hl][:, qb * 512:(qb + 1) * 512],
                                    start=True, stop=True)
                    pe_sc[c_] = ctx.inc(mm_, "PE")

                emit_scores(0)
                for c_ in range(nck):
                    v_sl = v_tm[:, c_ * 256 + hl * 128:
                                c_ * 256 + hl * 128 + 128]
                    ps = psum[2 + c_ % 2]
                    eb = exp_sb[c_ % 2]
                    pe_s = pe_sc[c_]
                    ctx.w(sc, "PE", pe_s)
                    if c_ >= 2:
                        ctx.w(sc, "PE", pe_use[c_ - 2])
                    ci = CIDX[(qb, c_)]
                    if p2 == 0 and hl == 0 and qb == 1 and c_ == 0:
                        t0 = tap("sc0", [128, 512])
                        if t0 is not None:
                            ctx.w(v, "PE", pe_s)
                            dtap = ctx.inc(v.tensor_copy(
                                out=uf[:, 0:512], in_=ps[:]), "DV")
                            ctx.w(sy, "DV", dtap)
                            ctx.dma_s(t0, uf[:, 0:512])
                            ctx.w(sc, "LS")
                    ac_exp[c_] = ctx.inc(
                        sc.activation(out=eb[:], in_=ps[:], func=AF.Exp,
                                      scale=col["sc"][:, 0:1],
                                      bias=btab[:, ci:ci + 1]), "AC")
                    if p2 == 0 and hl == 0 and qb == 1 and c_ == 0:
                        t0 = tap("exp0", [128, 512], BF16)
                        if t0 is not None:
                            ctx.w(sy, "AC", ac_exp[c_])
                            ctx.dma_s(t0, eb[:])
                            ctx.w(te, "LS")
                    if c_ + 1 < nck:
                        emit_scores(c_ + 1)
                    Dd = c_ - 8 - 4 * qb
                    if 0 <= Dd <= 3:
                        w_ = (Dd + 1) * 128
                        ctx.w(v, "AC", ac_exp[c_])
                        dv_msk = ctx.inc(v.tensor_tensor(
                            out=eb[:, 0:w_], in0=eb[:, 0:w_],
                            in1=dm3[:, (3 - Dd) * 128:512], op=ALU.mult),
                            "DV")
                        ctx.w(te, "DV", dv_msk)
                    else:
                        ctx.w(te, "AC", ac_exp[c_])
                    st_, sp_ = (c_ == 0), (c_ == nck - 1)
                    te.matmul(psum[4][:], v_sl, eb[:], start=st_, stop=sp_)
                    mm_ = te.matmul(prow[:], ones_col_bf[:], eb[:],
                                    start=st_, stop=sp_)
                    pe_use[c_] = ctx.inc(mm_, "PE")
                ctx.w(v, "PE", pe_use[nck - 1])
                if p2 == 0 and hl == 0 and qb == 1:
                    t0 = tap("pat0", [128, 512])
                    if t0 is not None:
                        dtap = ctx.inc(v.tensor_copy(
                            out=uf[:, 0:512], in_=psum[4][:]), "DV")
                        ctx.w(sy, "DV", dtap)
                        ctx.dma_s(t0, uf[:, 0:512])
                    t0 = tap("prow0", [1, 512])
                    if t0 is not None:
                        dtap = ctx.inc(v.tensor_copy(
                            out=uf[0:1, 512:1024], in_=prow[:]), "DV")
                        ctx.w(sy, "DV", dtap)
                        tgt_t = ctx.dma_s(t0, uf[0:1, 512:1024])
                        ctx.w(v, "LS", tgt_t)
                dv_i = ctx.inc(v.reciprocal(out=inv_sb[:], in_=prow[:]),
                               "DV")
                ctx.w(te, "DV", dv_i)
                pe_b = ctx.inc(te.matmul(psum[5][:], ones_row_f[:],
                                         inv_sb[:], start=True, stop=True),
                               "PE")
                ctx.w(v, "PE", pe_b)
                ctx.inc(v.tensor_copy(out=inv_b[:], in_=psum[5][:]), "DV")
                dv_at = ctx.inc(v.tensor_tensor(
                    out=attn_fm[hglob][:, qb * 512:(qb + 1) * 512],
                    in0=psum[4][:], in1=inv_b[:], op=ALU.mult), "DV")
                ctx.w(te, "DV", dv_at)

        emit_gamma(24)

    t = tap("attnfm", [128, 1024 * 16], BF16)
    if t is not None:
        ctx.w(sy, "DV")
        ctx.w(sy, "AC")
        for i in range(16):
            ctx.dma_s(t[:, i * 1024:(i + 1) * 1024], attn_fm[i][:])

    if _stop("attn"):
        return nc, taps

    emit_gamma(len(gamma_pieces))
    ctx.w(v, "AC")
    v.reduce_sum(col["gmax"][:, 0:1], gabs[:, 0:64], axis=AX.X)
    v.reduce_sum(col["gmax"][:, 1:2], gabs[:, 64:128], axis=AX.X)
    dv_g = ctx.inc(v.reduce_sum(col["tmp"][:, 0:1], gabs[:, 128:192],
                                axis=AX.X), "DV")
    ctx.w(te, "DV", dv_g)
    te.matmul(prow[:, 0:2], ones_col_f[:], col["gmax"][:, 0:2],
              start=True, stop=True)
    pe_g = ctx.inc(te.matmul(prow[:, 2:3], ones_col_f[:], col["tmp"][:, 0:1],
                             start=True, stop=True), "PE")
    ctx.w(v, "PE", pe_g)
    vf(v.tensor_copy(out=grow[:, 0:3], in_=prow[:, 0:3]))
    for i in range(3):
        vf(v.tensor_scalar(out=grow[:, i:i + 1], in0=grow[:, i:i + 1],
                           scalar1=1.0 / (C * HID), scalar2=1e-5,
                           op0=ALU.mult, op1=ALU.max))
    for i in range(3):
        v.tensor_scalar(out=grow[:, 4 + 2 * i:5 + 2 * i],
                        in0=grow[:, i:i + 1], scalar1=0.5, scalar2=None,
                        op0=ALU.mult)
        vf(v.tensor_scalar(out=grow[:, 5 + 2 * i:6 + 2 * i],
                           in0=grow[:, i:i + 1], scalar1=-0.5, scalar2=None,
                           op0=ALU.mult))
    dv_g2 = ctx.inc(v.reciprocal(out=grow[:, 10:13], in_=grow[:, 0:3]), "DV")
    ctx.w(v, "DV", dv_g2)
    ctx.w(te, "DV", dv_g2)
    te.matmul(psum[0][:, 0:6], ones_row_f[:], grow[:, 4:10],
              start=True, stop=True)
    pe_g2 = ctx.inc(te.matmul(psum[0][:, 6:8], ones_row_f[:], grow[:, 0:2],
                              start=True, stop=True), "PE")
    ctx.w(v, "PE", pe_g2)
    dv_g3 = ctx.inc(v.tensor_copy(out=gam[:], in_=psum[0][:, 0:8]), "DV")
    t = tap("gamma", [1, 16])
    if t is not None:
        ctx.w(sy, "DV", dv_g3)
        ctx.dma_s(t, grow[:])

    if _stop("gamma"):
        return nc, taps

    t = tap("mbf4", [1, 4 * C], BF16)
    if t is not None:
        ctx.w(sy, "DV")
        ctx.w(sy, "AC")
        ctx.dma_s(t, mbf[:])
    bcast_row(bc2, mbf[:, 2 * C:3 * C], ones_row_bf)
    ctx.w(g, "PE")
    lg_wo = 0
    for cch in range(16):
        lg_wo = ctx.dma_g(wo_sb[cch][:], woT[cch * 128:(cch + 1) * 128, :])
    wo_bank = [ctx.cnt["DV"], ctx.cnt["DV"]]
    ctx.w(sy, "PE")
    ctx.w(sy, "AC")
    dv_ep = {}
    spill_ls = {}
    for s in range(8):
        xn = xnew_t[s % 2]
        if s >= 2:
            ctx.w(sy, "DV", dv_ep[s - 2])
        if s >= 1:
            ctx.w(sy, "DV", dv_ep[s - 1])
        tgt_o = ctx.dma_s(xo_t[:], xb[(8 + s) * 128:(9 + s) * 128, :])
        ctx.w(te, "LG", lg_wo)
        for oc in range(4):
            pb = psum[oc % 2]
            ctx.w(te, "DV", wo_bank[oc % 2])
            if s >= 2 and oc == 0:
                ctx.w(v, "LS", spill_ls[s - 2])
            for cch in range(16):
                mm_ = te.matmul(pb[:],
                                attn_fm[cch][:, s * 128:(s + 1) * 128],
                                wo_sb[cch][:, oc * 512:(oc + 1) * 512],
                                start=(cch == 0), stop=(cch == 15))
            pe_mm = ctx.inc(mm_, "PE")
            ctx.w(v, "PE", pe_mm)
            if s == 0 and oc == 0:
                t0 = tap("wops", [128, 512])
                if t0 is not None:
                    dtp = ctx.inc(v.tensor_copy(out=ef[:], in_=pb[:]), "DV")
                    ctx.w(sy, "DV", dtp)
                    ctx.dma_s(t0, ef[:])
                t0 = tap("bc2t", [128, 2048])
                if t0 is not None:
                    ctx.dma_s(t0, bc2[:])
            wo_bank[oc % 2] = ctx.inc(v.tensor_tensor(
                out=xn[:, oc * 512:(oc + 1) * 512], in0=pb[:],
                in1=bc2[:, oc * 512:(oc + 1) * 512], op=ALU.mult), "DV")
        ctx.w(v, "LS", tgt_o)
        dv_x = ctx.inc(v.tensor_tensor(out=xn[:], in0=xn[:], in1=xo_t[:],
                                       op=ALU.add), "DV")
        dv_ep[s] = dv_x
        ctx.w(sy, "DV", dv_x)
        spill_ls[s] = ctx.dma_s(xnew_hbm[s * 128:(s + 1) * 128, :], xn[:])

    t = tap("xnew", [1024, C])
    if t is not None:
        ctx.w(sy, "LS", ctx.cnt["LS"])
        ctx.dma_s(t, xnew_hbm)

    if _stop("xnew"):
        return nc, taps

    bcast_row(bcA2, mbf[:, 0:C], ones_row_bf)
    bcast_row(bcB2, mbf[:, C:2 * C], ones_row_bf)
    ctx.w(sy, "LS", ctx.cnt["LS"])
    dv_q2 = {}
    for s in range(8):
        xn = xnew_t[s % 2]
        if s >= 2:
            ctx.w(sy, "DV", dv_q2[s - 2])
        tgt = ctx.dma_s(xn[:], xnew_hbm[s * 128:(s + 1) * 128, :])
        ctx.w(v, "LS", tgt)
        emit_ln_apply(xn[:], bcA2, bcB2, trash_bf[:])
        v.tensor_tensor(out=uf[:], in0=uf[:], in1=bcB2[:], op=ALU.add)
        vf(v.reduce_max(col["gmax"][:, 0:1], uf[:], axis=AX.X,
                        apply_absolute_value=True))
        vf(v.tensor_scalar(out=col["gmax"][:, 0:1], in0=col["gmax"][:, 0:1],
                           scalar1=1e-5, scalar2=None, op0=ALU.max))
        vf(v.reciprocal(out=col["srow"][:, 0:1], in_=col["gmax"][:, 0:1]))
        vf(v.tensor_scalar(out=col["srow"][:, 0:1], in0=col["srow"][:, 0:1],
                           scalar1=127.0, scalar2=None, op0=ALU.mult))
        vf(v.tensor_scalar(out=col["invsr"][:, 0:1], in0=col["gmax"][:, 0:1],
                           scalar1=1.0 / 127.0, scalar2=None, op0=ALU.mult))
        v.tensor_scalar(out=uf[:], in0=uf[:], scalar1=col["srow"][:, 0:1],
                        scalar2=128.0, op0=ALU.mult, op1=ALU.add)
        for ch in range(4):
            usl = uf[:, ch * 512:(ch + 1) * 512]
            v.tensor_copy(out=u32[:], in_=usl)
            v.tensor_tensor(out=ef[:], in0=usl, in1=u32[:], op=ALU.subtract)
            v.scalar_tensor_tensor(out=ef[:], in0=ef[:],
                                   scalar=col["half"][:, 0:1], in1=u32[:],
                                   op0=ALU.is_ge, op1=ALU.add)
            v.tensor_scalar(out=trash_bf[:, ch * 512:(ch + 1) * 512],
                            in0=ef[:], scalar1=-128.0,
                            scalar2=col["invsr"][:, 0:1],
                            op0=ALU.add, op1=ALU.mult)
        ctx.inc(v.engine_nop(), "DV")
        dv_q2[s] = emit_transpose16(trash_bf, lambda cch: q_h2[cch],
                                    s * 128)

    t = tap("qh2", [128, 1024 * 16], BF16)
    if t is not None:
        ctx.w(sy, "DV")
        for i in range(16):
            ctx.dma_s(t[:, i * 1024:(i + 1) * 1024], q_h2[i][:])

    if _stop("qh2"):
        return nc, taps

    ctx.w(g, "PE")
    ctx.w(g, "DV")
    ctx.dma_g(brow_g[:], gb_in)
    lg_b = ctx.dma_g(brow_v[:], vb_in)
    ctx.w(v, "LG", lg_b)
    ctx.w(v, "DV", dv_g2)
    v.tensor_scalar(out=brow_g[:], in0=brow_g[:], scalar1=grow[:, 10:11],
                    scalar2=None, op0=ALU.mult)
    v.tensor_scalar(out=brow_v[:], in0=brow_v[:], scalar1=grow[:, 11:12],
                    scalar2=None, op0=ALU.mult)
    ctx.inc(v.tensor_scalar(out=brow_o[:], in0=brow_o[:],
                            scalar1=grow[:, 12:13], scalar2=None,
                            op0=ALU.mult), "DV")

    def emit_ternary(dst_ap, src_ap, cpos, cneg):
        v.tensor_scalar(out=wq_a[:], in0=src_ap,
                        scalar1=gam[:, cpos:cpos + 1], scalar2=None,
                        op0=ALU.is_ge)
        v.tensor_scalar(out=wq_b[:], in0=src_ap,
                        scalar1=gam[:, cneg:cneg + 1], scalar2=None,
                        op0=ALU.is_le)
        return v.tensor_tensor(out=dst_ap, in0=wq_a[:], in1=wq_b[:],
                               op=ALU.subtract)

    ctx.w(v, "PE")
    pe_gv = {}
    spill6 = {}
    for hc in range(16):
        if hc >= 1:
            ctx.w(g, "PE", pe_gv[hc - 1])
        gsrc = gwT[:, hc * 512:(hc + 1) * 512].rearrange(
            "(k p) n -> p k n", p=128)
        vsrc = vwT[:, hc * 512:(hc + 1) * 512].rearrange(
            "(k p) n -> p k n", p=128)
        ctx.dma_g(gw_bf[:], gsrc)
        lg_w = ctx.dma_g(vw_bf[:], vsrc)
        ctx.w(v, "LG", lg_w)
        ctx.w(v, "PE", pe_gv.get(hc - 1, 0))
        for cch in range(16):
            emit_ternary(wg_op[:, cch * 512:(cch + 1) * 512],
                         gw_bf[:, cch * 512:(cch + 1) * 512], 0, 1)
            last = emit_ternary(wv_op[:, cch * 512:(cch + 1) * 512],
                                vw_bf[:, cch * 512:(cch + 1) * 512], 2, 3)
        dv_wq = ctx.inc(last, "DV")
        ctx.w(te, "DV", dv_wq)
        for s in range(8):
            ctx.w(te, "AC")
            ctx.w(te, "DV")
            for cch in range(16):
                te.matmul(psum[0][:], q_h2[cch][:, s * 128:(s + 1) * 128],
                          wg_op[:, cch * 512:(cch + 1) * 512],
                          start=(cch == 0), stop=False)
            mm_ = te.matmul(psum[0][:], ones_row_bf[:],
                            brow_g[:, hc * 512:(hc + 1) * 512],
                            start=False, stop=True)
            pe_gm = ctx.inc(mm_, "PE")
            ctx.w(sc, "PE", pe_gm)
            ctx.w(sc, "DV")
            acg = ctx.inc(sc.activation(out=gate_s[:], in_=psum[0][:],
                                        func=AF.Silu, scale=gam[:, 6:7]),
                          "AC")
            for cch in range(16):
                te.matmul(psum[1][:], q_h2[cch][:, s * 128:(s + 1) * 128],
                          wv_op[:, cch * 512:(cch + 1) * 512],
                          start=(cch == 0), stop=False)
            mm_ = te.matmul(psum[1][:], ones_row_bf[:],
                            brow_v[:, hc * 512:(hc + 1) * 512],
                            start=False, stop=True)
            pe_vm = ctx.inc(mm_, "PE")
            pe_gv[hc] = pe_vm
            ctx.w(v, "PE", pe_vm)
            ctx.w(v, "AC", acg)
            it6 = hc * 8 + s
            gvb = gv_s if it6 % 2 == 0 else ef
            if it6 >= 2:
                ctx.w(v, "LS", spill6.get(it6 - 2, 0))
            v.scalar_tensor_tensor(out=gvb[:], in0=psum[1][:],
                                   scalar=gam[:, 7:8], in1=gate_s[:],
                                   op0=ALU.mult, op1=ALU.mult)
            if hc == 0:
                dvm = ctx.inc(v.reduce_max(gvmax[:, s:s + 1], gvb[:],
                                           axis=AX.X,
                                           apply_absolute_value=True), "DV")
            else:
                vf(v.reduce_max(col["tmp"][:, 0:1], gvb[:], axis=AX.X,
                                apply_absolute_value=True))
                dvm = ctx.inc(v.tensor_tensor(out=gvmax[:, s:s + 1],
                                              in0=gvmax[:, s:s + 1],
                                              in1=col["tmp"][:, 0:1],
                                              op=ALU.max), "DV")
            ctx.w(sy, "DV", dvm)
            spill6[it6] = ctx.dma_s(gv_hbm[s * 128:(s + 1) * 128,
                                           hc * 512:(hc + 1) * 512], gvb[:])

    t = tap("gv", [1024, HID])
    if t is not None:
        ctx.w(sy, "LS", ctx.cnt["LS"])
        ctx.dma_s(t, gv_hbm)

    ctx.w(v, "LS", ctx.cnt["LS"])
    vf(v.tensor_scalar(out=gvmax[:], in0=gvmax[:], scalar1=1e-5,
                       scalar2=None, op0=ALU.max))
    vf(v.reciprocal(out=s2c[:], in_=gvmax[:]))
    vf(v.tensor_scalar(out=s2c[:], in0=s2c[:], scalar1=127.0, scalar2=None,
                       op0=ALU.mult))
    vf(v.tensor_scalar(out=is2c[:], in0=gvmax[:],
                       scalar1=1.0 / 127.0, scalar2=None, op0=ALU.mult))

    if _stop("gv"):
        return nc, taps

    bcast_row(bc2, mbf[:, 3 * C:4 * C], ones_row_bf)
    vf(v.tensor_scalar(out=col["tmp"][:, 0:1], in0=gam[:, 4:5],
                       scalar1=2.0, scalar2=None, op0=ALU.mult))
    dv_bc2 = ctx.inc(v.tensor_scalar(out=bc2[:], in0=bc2[:],
                                     scalar1=col["tmp"][:, 0:1],
                                     scalar2=None, op0=ALU.mult), "DV")

    for hq in range(4):
        dv_x2 = {}
        ctx.w(sy, "DV")
        for s in range(8):
            for hcl in range(4):
                i = s * 4 + hcl
                xqb = xq[i % 2]
                if i >= 2:
                    ctx.w(sy, "DV", dv_x2.get(i - 2, 0))
                tgt = ctx.dma_s(xqb[:],
                                gv_hbm[s * 128:(s + 1) * 128,
                                       hq * 2048 + hcl * 512:
                                       hq * 2048 + (hcl + 1) * 512])
                ctx.w(v, "LS", tgt)
                v.tensor_scalar(out=uf[:, 0:512], in0=xqb[:],
                                scalar1=s2c[:, s:s + 1], scalar2=128.0,
                                op0=ALU.mult, op1=ALU.add)
                v.tensor_copy(out=u32[:], in_=uf[:, 0:512])
                v.tensor_tensor(out=ef[:], in0=uf[:, 0:512], in1=u32[:],
                                op=ALU.subtract)
                v.scalar_tensor_tensor(out=ef[:], in0=ef[:],
                                       scalar=col["half"][:, 0:1],
                                       in1=u32[:], op0=ALU.is_ge,
                                       op1=ALU.add)
                dvx = ctx.inc(v.tensor_scalar(out=uf[:, 512:1024],
                                              in0=ef[:], scalar1=-128.0,
                                              scalar2=is2c[:, s:s + 1],
                                              op0=ALU.add, op1=ALU.mult),
                              "DV")
                ctx.w(te, "DV", dvx)
                for q4 in range(4):
                    mm_ = te.transpose(
                        psum[5][:, q4 * 128:(q4 + 1) * 128],
                        uf[:, 512 + q4 * 128:512 + (q4 + 1) * 128],
                        identf[:])
                pe_t = ctx.inc(mm_, "PE")
                ctx.w(v, "PE", pe_t)
                for q4 in range(4):
                    hid128 = hcl * 4 + q4
                    dst = bufA[hid128 // 2]
                    coff = (hid128 % 2) * 1024 + s * 128
                    last = v.tensor_copy(out=dst[:, coff:coff + 128],
                                         in_=psum[5][:, q4 * 128:
                                                     (q4 + 1) * 128])
                dv_x2[i] = ctx.inc(last, "DV")
                ctx.w(te, "DV", dv_x2[i])

        for oc in range(4):
            dv_wo = 0
            ctx.w(v, "PE")
            for hcl in range(16):
                src = owT[hq * 2048 + hcl * 128: hq * 2048 + (hcl + 1) * 128,
                          oc * 512:(oc + 1) * 512]
                wb = gstr[hcl % 2]
                ctx.w(g, "DV")
                lg_o = ctx.dma_g(wb[:, 0:512], src)
                ctx.w(v, "LG", lg_o)
                last = emit_ternary(wq_col[:, hcl * 512:(hcl + 1) * 512],
                                    wb[:, 0:512], 4, 5)
                dv_wo = ctx.inc(last, "DV")
            ctx.w(te, "DV", dv_wo)
            for tg in range(2):
                for s4 in range(4):
                    s = tg * 4 + s4
                    pb = psum[s4]
                    for hcl in range(16):
                        mm_ = te.matmul(pb[:], lhs_ := bufA[hcl // 2][
                            :, (hcl % 2) * 1024 + s * 128:
                            (hcl % 2) * 1024 + s * 128 + 128],
                            wq_col[:, hcl * 512:(hcl + 1) * 512],
                            start=(hcl == 0),
                            stop=(hq < 3 and hcl == 15))
                    if hq == 3:
                        mm_ = te.matmul(pb[:], ones_row_bf[:],
                                        brow_o[:, oc * 512:(oc + 1) * 512],
                                        start=False, stop=True)
                    pe_o = ctx.inc(mm_, "PE")
                    ctx.w(v, "PE", pe_o)
                    if s == 7 and oc == 0:
                        t0 = tap(f"ps7q{hq}", [128, 512])
                        if t0 is not None:
                            dtp = ctx.inc(v.tensor_copy(out=ef[:],
                                                        in_=pb[:]), "DV")
                            ctx.w(sy, "DV", dtp)
                            ctx.dma_s(t0, ef[:])
                    if hq == 0:
                        dv_a = ctx.inc(v.tensor_copy(
                            out=acc[s][:, oc * 512:(oc + 1) * 512],
                            in_=pb[:]), "DV")
                    else:
                        dv_a = ctx.inc(v.tensor_tensor(
                            out=acc[s][:, oc * 512:(oc + 1) * 512],
                            in0=acc[s][:, oc * 512:(oc + 1) * 512],
                            in1=pb[:], op=ALU.add), "DV")
                    ctx.w(te, "DV", dv_a)

    t = tap("acc0", [128, 2048])
    if t is not None:
        ctx.w(sy, "DV")
        ctx.dma_s(t, acc[0][:])
    t = tap("acc7", [128, 2048])
    if t is not None:
        ctx.w(sy, "DV")
        ctx.dma_s(t, acc[7][:])
    t = tap("x2fm0", [128, 2048], BF16)
    if t is not None:
        ctx.w(sy, "DV")
        ctx.dma_s(t, bufA[0][:])
    dv_y = 0
    ctx.w(sy, "DV")
    for s in range(8):
        if s >= 1:
            ctx.w(sy, "DV", dv_y)
        tgt = ctx.dma_s(xo_t[:], xnew_hbm[s * 128:(s + 1) * 128, :])
        ctx.w(v, "LS", tgt)
        ctx.w(v, "DV", dv_bc2)
        v.tensor_tensor(out=uf[:], in0=acc[s][:], in1=bc2[:], op=ALU.mult)
        dv_y = ctx.inc(v.tensor_tensor(out=uf[:], in0=uf[:], in1=xo_t[:],
                                       op=ALU.add), "DV")
        ctx.w(sy, "DV", dv_y)
        ctx.dma_s(y_out[s * 128:(s + 1) * 128, :], uf[:])
        ctx.w(v, "LS")

    sy.wait_ge(ctx.sem["LS"], ctx.cnt["LS"])
    return nc, taps


def _host_prep(L0_attn_x, L0_attn_W_qkv, L0_attn_W_o, L0_ffn_gate_w,
               L0_ffn_gate_b, L0_ffn_val_w, L0_ffn_val_b, L0_ffn_out_w,
               L0_ffn_out_b, L0_c, L0_wmod):
    import ml_dtypes
    x = np.ascontiguousarray(np.asarray(L0_attn_x, dtype=np.float32))
    wqkv = np.asarray(L0_attn_W_qkv, dtype=np.float32)
    perm = np.empty(128, dtype=np.int64)
    perm[0:64] = np.arange(64) * 2
    perm[64:128] = np.arange(64) * 2 + 1
    rows = np.arange(3 * C)
    sec, rem = rows // C, rows % C
    h_, d_ = rem // D, rem % D
    newsrc = np.where(sec < 2, sec * C + h_ * D + perm[d_], rows)
    wqkvT = np.ascontiguousarray(wqkv[newsrc].T)
    woT = np.ascontiguousarray(np.asarray(L0_attn_W_o, np.float32).T)
    gwT = np.ascontiguousarray(np.asarray(L0_ffn_gate_w, np.float32).T)
    vwT = np.ascontiguousarray(np.asarray(L0_ffn_val_w, np.float32).T)
    owT = np.ascontiguousarray(np.asarray(L0_ffn_out_w, np.float32).T)
    wmod = np.ascontiguousarray(np.asarray(L0_wmod, np.float32))
    carr = np.asarray(L0_c, dtype=np.float32)
    gb = np.ascontiguousarray(np.asarray(L0_ffn_gate_b,
                                         np.float32).reshape(1, HID))
    vb = np.ascontiguousarray(np.asarray(L0_ffn_val_b,
                                         np.float32).reshape(1, HID))
    ob = np.ascontiguousarray(np.asarray(L0_ffn_out_b,
                                         np.float32).reshape(1, C))
    identb = np.eye(128, dtype=ml_dtypes.bfloat16)
    identf = np.eye(128, dtype=np.float32)

    in_maps = []
    for core in range(8):
        b, half = core // 2, core % 2
        if half == 1:
            xb_eff = x[b]
        else:
            xb_eff = np.concatenate([x[b, 1024:2048], x[b, 0:1024]], axis=0)
        bt = np.full((128, NCOMB), EXP_SHIFT, dtype=np.float32)
        if half == 0:
            for j, (qb, c_) in enumerate(COMBOS):
                if c_ < 8:
                    bt[:, j] = DEAD
        pos = np.zeros((128, 16), dtype=np.float32)
        for s in range(16):
            if half == 0:
                base = (s - 8) * 128 if s >= 8 else s * 128
            else:
                base = s * 128
            pos[:, s] = base + np.arange(128)
        ccol_ = np.ascontiguousarray(carr[b].reshape(16, 128).T)
        in_maps.append({
            "xb": np.ascontiguousarray(xb_eff), "wqkvT": wqkvT, "woT": woT,
            "gwT": gwT, "vwT": vwT, "owT": owT, "wmod": wmod, "ccol": ccol_,
            "gb_in": gb, "vb_in": vb, "ob_in": ob, "biastab": bt,
            "pos_col": pos, "identb": identb, "identf": identf,
        })
    return in_maps


def kernel(**inputs):
    in_maps = _host_prep(**inputs)
    nc, _ = build()
    res = run_bass_kernel_spmd(nc, in_maps, core_ids=list(range(8)))
    out = np.empty((B, T, C), dtype=np.float32)
    for core in range(8):
        b, half = core // 2, core % 2
        out[b, half * 1024:(half + 1) * 1024] = res.results[core]["y"]
    return out


# version 05
# speedup vs baseline: 1.4413x; 1.2902x over previous
import numpy as np
import concourse.bass as bass
import concourse.mybir as mybir
from concourse.bass_utils import run_bass_kernel_spmd

F32 = mybir.dt.float32
BF16 = mybir.dt.bfloat16
I32 = mybir.dt.int32
ALU = mybir.AluOpType
AF = mybir.ActivationFunctionType
AX = mybir.AxisListType

B, T, C, H, D = 4, 2048, 2048, 16, 128
HID = 4 * C
NT = 16
LN_EPS = 1e-5
RMS_EPS = 1.1920929e-07
EXP_SHIFT = -11.5
DEAD = -1e9
SCALE = float(1.0 / np.sqrt(np.float32(128.0)))
TWO_PI = float(2 * np.pi)
KB = 1024

COMBOS = [(qb, c) for qb in (0, 1) for c in range(12 if qb == 0 else 16)]
NCOMB = len(COMBOS)
CIDX = {qc: i for i, qc in enumerate(COMBOS)}


class Ctx:

    def __init__(self, nc):
        self.nc = nc
        self.sem = {}
        self.cnt = {}
        for name in ("LS", "LG", "PE", "DV", "AC", "GP"):
            self.sem[name] = nc.alloc_semaphore(name)
            self.cnt[name] = 0

    def inc(self, inst, s, n=1):
        inst.then_inc(self.sem[s], n)
        self.cnt[s] += n
        return self.cnt[s]

    def w(self, eng, s, val=None):
        if val is None:
            val = self.cnt[s]
        eng.wait_ge(self.sem[s], val)

    def dma_s(self, out, in_):
        return self.inc(self.nc.sync.dma_start(out=out, in_=in_), "LS", 16)

    def dma_g(self, out, in_):
        return self.inc(self.nc.gpsimd.dma_start(out=out, in_=in_), "LG", 16)


def build(debug_taps=(), stop_phase=None):
    nc = bass.Bass("TRN2", target_bir_lowering=False, debug=False,
                   detect_race_conditions=False)
    ctx = Ctx(nc)
    v, sc, te, g, sy = nc.vector, nc.scalar, nc.tensor, nc.gpsimd, nc.sync

    xb = nc.dram_tensor("xb", [T, C], F32, kind="ExternalInput").ap()
    wqkvT = nc.dram_tensor("wqkvT", [C, 3 * C], F32, kind="ExternalInput").ap()
    woT = nc.dram_tensor("woT", [C, C], F32, kind="ExternalInput").ap()
    gwT = nc.dram_tensor("gwT", [C, HID], F32, kind="ExternalInput").ap()
    vwT = nc.dram_tensor("vwT", [C, HID], F32, kind="ExternalInput").ap()
    owT = nc.dram_tensor("owT", [HID, C], F32, kind="ExternalInput").ap()
    wmod = nc.dram_tensor("wmod", [C, 6 * C], F32, kind="ExternalInput").ap()
    ccol = nc.dram_tensor("ccol", [128, 16], F32, kind="ExternalInput").ap()
    gb_in = nc.dram_tensor("gb_in", [1, HID], F32, kind="ExternalInput").ap()
    vb_in = nc.dram_tensor("vb_in", [1, HID], F32, kind="ExternalInput").ap()
    ob_in = nc.dram_tensor("ob_in", [1, C], F32, kind="ExternalInput").ap()
    biastab = nc.dram_tensor("biastab", [128, NCOMB], F32,
                             kind="ExternalInput").ap()
    pos_in = nc.dram_tensor("pos_col", [128, 16], F32,
                            kind="ExternalInput").ap()
    identb_in = nc.dram_tensor("identb", [128, 128], BF16,
                               kind="ExternalInput").ap()
    identf_in = nc.dram_tensor("identf", [128, 128], F32,
                               kind="ExternalInput").ap()

    y_out = nc.dram_tensor("y", [1024, C], F32, kind="ExternalOutput").ap()
    gv_hbm = nc.dram_tensor("gv_hbm", [1024, HID], F32, kind="Internal").ap()
    xnew_hbm = nc.dram_tensor("xnew_hbm", [1024, C], F32, kind="Internal").ap()

    taps = {}

    def tap(name, shape, dtype=F32):
        if name in debug_taps:
            taps[name] = nc.dram_tensor("tap_" + name, shape, dtype,
                                        kind="ExternalOutput").ap()
        return taps.get(name)

    ARENA_SZ = 204 * KB
    arena = nc.alloc_sbuf_tensor("arena", [128, ARENA_SZ], mybir.dt.uint8)
    A0 = nc.lookup_mloc(arena).addr

    _static_iv = []

    def place(name, shape, dtype, off):
        h_ = nc.alloc_sbuf_tensor_at(name, shape, dtype, offset=A0 + off)
        nbytes = int(np.prod(shape[1:])) * mybir.dt.size(dtype)
        assert off + nbytes <= ARENA_SZ, (name, off, nbytes)
        if off + nbytes <= 68 * KB:
            for o0, o1, n0 in _static_iv:
                assert off >= o1 or off + nbytes <= o0, (name, n0, off)
            _static_iv.append((off, off + nbytes, name))
        return h_

    uf = place("uf", [128, 2048], F32, 0)
    u32 = place("u32", [128, 512], I32, 8 * KB)
    ef = place("ef", [128, 512], F32, 10 * KB)
    trash_bf = place("trash_bf", [128, 2048], BF16, 12 * KB)
    mbf = place("mbf", [1, 4 * C], BF16, 16 * KB)
    sin_tm = place("sin_tm", [128, 1024], BF16, 32 * KB)
    cos_tm = place("cos_tm", [128, 1024], BF16, 34 * KB)
    gstr = [place(f"gstr{i}", [128, 2048], BF16, 36 * KB + i * 4 * KB)
            for i in range(2)]
    brow_o = place("brow_o", [1, C], BF16, 44 * KB)
    exp_sb = [place(f"exp{i}", [128, 512], BF16, 48 * KB + i * KB)
              for i in range(2)]
    inv_sb = place("inv_sb", [1, 512], F32, 50 * KB)
    inv_b = place("inv_b", [128, 512], BF16, 52 * KB)
    gate_s = place("gate_s", [128, 512], F32, 53 * KB)
    gv_s = place("gv_s", [128, 512], F32, 55 * KB)
    xq = [place(f"xq{i}", [128, 512], F32, 57 * KB + i * 2 * KB)
          for i in range(2)]
    wq_a = place("wq_a", [128, 512], BF16, 61 * KB)
    wq_b = place("wq_b", [128, 512], BF16, 62 * KB)
    ident = place("ident_sb", [128, 128], BF16, 63 * KB)
    identf = place("identf_sb", [128, 128], F32, 63 * KB + 256)
    tri = place("tri", [128, 128], BF16, 63 * KB + 768)
    btab = place("btab", [128, NCOMB], F32, 64 * KB)
    pcol = place("pcol", [128, 16], F32, 64 * KB + 128)
    cc_sb = place("cc_sb", [128, 16], F32, 64 * KB + 192)
    silu_c = place("silu_c", [128, 16], BF16, 64 * KB + 256)
    silu_cf = nc.alloc_sbuf_tensor_at("silu_cf", [128, 16], F32,
                                      offset=A0 + 8 * KB + 1024)
    gate_s2 = nc.alloc_sbuf_tensor_at("gate_s2", [128, 512], F32,
                                      offset=A0 + 48 * KB)
    invfq = place("invfq", [1, 64], F32, 64 * KB + 288)
    ivb = place("ivb", [128, 64], F32, 64 * KB + 544)
    gabs = place("gabs", [128, 192], F32, 64 * KB + 800)
    grow = place("grow", [1, 16], F32, 64 * KB + 1568)
    gam = place("gam", [128, 8], F32, 64 * KB + 1632)
    gvmax = place("gvmax", [128, 8], F32, 64 * KB + 1664)
    s2c = place("s2c", [128, 8], F32, 64 * KB + 1696)
    is2c = place("is2c", [128, 8], F32, 64 * KB + 1728)
    rms_c = place("rms_c", [128, 12], F32, 64 * KB + 1760)
    ones_col_bf = place("ones_col_bf", [128, 1], BF16, 64 * KB + 1824)
    ones_col_f = place("ones_col_f", [128, 1], F32, 64 * KB + 1856)
    ones_row_bf = place("ones_row_bf", [1, 128], BF16, 64 * KB + 1888)
    ones_row_f = place("ones_row_f", [1, 128], F32, 64 * KB + 2144)
    col = {}
    for i, name in enumerate(("sum", "mean", "var", "invs", "half", "sc",
                              "c2pi", "gmax", "srow", "invsr", "tmp")):
        col[name] = place("col_" + name, [128, 2], F32,
                          64 * KB + 2688 + i * 32)
    dm3 = place("dm3", [128, 512], BF16, 64 * KB + 3040)

    B0 = 68 * KB
    mrow = place("mrow", [1, 6 * C], F32, B0)
    bufA = [place(f"bufA{i}", [128, 2048], BF16, B0 + i * 4 * KB)
            for i in range(16)]
    attn_fm = [place(f"attnfm{i}", [128, 1024], BF16,
                     B0 + 64 * KB + i * 2 * KB) for i in range(16)]
    x_t = [place(f"x_t{i}", [128, 2048], F32, B0 + 64 * KB + i * 8 * KB)
           for i in range(2)]
    bcA1 = place("bcA1", [128, 2048], F32, B0 + 96 * KB)
    bcB1 = place("bcB1", [128, 2048], F32, B0 + 104 * KB)
    wslice = place("wslice", [128, 4096], BF16, B0 + 96 * KB)
    q_tm = place("q_tm", [128, 2048], BF16, B0 + 104 * KB)
    k_tm = place("k_tm", [128, 4096], BF16, B0 + 108 * KB)
    v_tm = place("v_tm", [128, 4096], BF16, B0 + 116 * KB)
    q_fm = [place(f"qfm{i}", [128, 1024], BF16, B0 + 124 * KB + i * 2 * KB)
            for i in range(2)]
    k_fm = [place(f"kfm{i}", [128, 2048], BF16, B0 + 128 * KB + i * 4 * KB)
            for i in range(2)]
    xnew_t = [place(f"xnew{i}", [128, 2048], F32, B0 + 96 * KB + i * 8 * KB)
              for i in range(2)]
    xo_t = place("xo_t", [128, 2048], F32, B0 + 112 * KB)
    bc2 = place("bc2", [128, 2048], F32, B0 + 120 * KB)
    bcA2 = place("bcA2", [128, 2048], F32, B0 + 120 * KB)
    bcB2 = place("bcB2", [128, 2048], F32, B0 + 128 * KB)
    gw_bf = place("gw_bf", [128, 8192], BF16, B0 + 96 * KB)
    vw_bf = place("vw_bf", [128, 8192], BF16, B0 + 112 * KB)
    wg_op = place("wg_op", [128, 8192], BF16, B0 + 0)
    wv_op = place("wv_op", [128, 8192], BF16, B0 + 16 * KB)
    brow_g = place("brow_g", [1, HID], BF16, B0 + 32 * KB)
    brow_v = place("brow_v", [1, HID], BF16, B0 + 48 * KB)
    acc = [place(f"acc{i}", [128, 2048], F32, B0 + 32 * KB + i * 8 * KB)
           for i in range(8)]
    wq_col = place("wq_col", [128, 8192], BF16, B0 + 96 * KB)

    psum = [nc.alloc_psum_tensor(f"pb{i}", [128, 512], F32) for i in range(6)]
    prow = nc.alloc_psum_tensor("prow", [1, 512], F32)
    ptr = nc.alloc_psum_tensor("ptr", [128, 1024], BF16)

    h1_fm = bufA
    wo_sb = bufA
    q_h2 = attn_fm

    def vf(inst):
        ctx.w(v, "DV", ctx.inc(inst, "DV"))

    def _stop(ph):
        if stop_phase == ph:
            sy.wait_ge(ctx.sem["LS"], ctx.cnt["LS"])
            return True
        return False


    ctx.dma_s(ident[:], identb_in)
    ctx.dma_s(identf[:], identf_in)
    ctx.dma_s(btab[:], biastab)
    ctx.dma_s(pcol[:], pos_in)
    ld0 = ctx.dma_s(cc_sb[:], ccol)
    ctx.dma_g(brow_o[:], ob_in)

    v.memset(ones_col_bf[:], 1.0)
    v.memset(ones_col_f[:], 1.0)
    v.memset(ones_row_bf[:], 1.0)
    v.memset(ones_row_f[:], 1.0)
    v.memset(col["half"][:], 0.5)
    v.memset(col["sc"][:], SCALE)
    v.memset(col["c2pi"][:], TWO_PI)
    dv_ms = ctx.inc(v.memset(trash_bf[:], 0.0), "DV")

    g.memset(tri[:], 1.0)
    gp_t = ctx.inc(g.engine_nop(), "GP")
    g.wait_ge(ctx.sem["GP"], gp_t)
    g.affine_select(tri[:], tri[:], pattern=[[1, 128]],
                    compare_op=ALU.is_ge, fill=0.0, base=0,
                    channel_multiplier=-1)
    g.iota(invfq[:], pattern=[[1, 64]], base=0, channel_multiplier=0,
           allow_small_or_imprecise_dtypes=True)
    gp0 = ctx.inc(g.engine_nop(), "GP")

    ctx.w(v, "GP", gp0)
    vf(v.memset(dm3[:], 0.0))
    vf(v.tensor_copy(out=dm3[:, 384:512], in_=tri[:]))
    v.memset(col["tmp"][:], float(-np.log(10000.0) / 64.0))
    dv0 = ctx.inc(v.engine_nop(), "DV")
    ctx.w(sc, "GP", gp0)
    ctx.w(sc, "DV", dv0)
    ac0 = ctx.inc(sc.activation(out=invfq[:], in_=invfq[:], func=AF.Exp,
                                scale=col["tmp"][0:1, 0:1]), "AC")
    ctx.w(te, "AC", ac0)
    pe0 = ctx.inc(te.matmul(psum[0][:, 0:64], ones_row_f[:], invfq[:],
                            start=True, stop=True), "PE")
    ctx.w(v, "PE", pe0)
    ctx.w(v, "LS", ld0)
    dv1 = ctx.inc(v.tensor_copy(out=ivb[:], in_=psum[0][:, 0:64]), "DV")

    def emit_sincos(dst, s, phase_add):
        ctx.w(v, "AC")
        v.tensor_scalar(out=uf[:, 64:128], in0=ivb[:],
                        scalar1=pcol[:, s:s + 1], scalar2=None, op0=ALU.mult)
        v.tensor_scalar(out=uf[:, 0:64], in0=uf[:, 64:128],
                        scalar1=float(1.0 / TWO_PI), scalar2=phase_add,
                        op0=ALU.mult, op1=ALU.add)
        v.tensor_copy(out=u32[:, 0:64], in_=uf[:, 0:64])
        v.tensor_tensor(out=ef[:, 0:64], in0=uf[:, 0:64], in1=u32[:, 0:64],
                        op=ALU.subtract)
        v.scalar_tensor_tensor(out=ef[:, 0:64], in0=ef[:, 0:64],
                               scalar=col["half"][:, 0:1], in1=u32[:, 0:64],
                               op0=ALU.is_ge, op1=ALU.add)
        dvx = ctx.inc(v.tensor_tensor(out=uf[:, 0:64], in0=uf[:, 0:64],
                                      in1=ef[:, 0:64], op=ALU.subtract),
                      "DV")
        ctx.w(sc, "DV", dvx)
        return ctx.inc(sc.activation(out=dst, in_=uf[:, 0:64], func=AF.Sin,
                                     scale=col["c2pi"][:, 0:1]), "AC")

    ctx.w(v, "DV", dv1)
    ac_tab = 0
    for s in range(16):
        emit_sincos(sin_tm[:, s * 64:(s + 1) * 64], s, 0.0)
        ac_tab = emit_sincos(cos_tm[:, s * 64:(s + 1) * 64], s, 0.25)

    t = tap("sin", [128, 1024], BF16)
    if t is not None:
        ctx.w(sy, "AC", ac_tab)
        ctx.dma_s(t, sin_tm[:])
    t = tap("cos", [128, 1024], BF16)
    if t is not None:
        ctx.w(sy, "AC", ac_tab)
        ctx.dma_s(t, cos_tm[:])

    if _stop("tables"):
        return nc, taps

    ctx.w(sc, "LS", ld0)
    sc.activation(out=silu_c[:], in_=cc_sb[:], func=AF.Silu)
    ac_sc = ctx.inc(sc.activation(out=silu_cf[:], in_=cc_sb[:],
                                  func=AF.Silu), "AC")
    ctx.w(te, "AC", ac_sc)
    ctx.w(te, "DV")
    pe_tile = {}
    dv_mod = 0
    ring_bf = [gstr[0], gstr[1], None, None]
    ring_f = [x_t[0], x_t[1], bcA1, bcB1]

    def _wmod_buf(i, gate_jb):
        if gate_jb:
            return ring_f[i % 4]
        if i % 4 < 2:
            return gstr[i % 4]
        return None

    wm_dma = {}

    def _emit_wmod_dma(i):
        jb, ck = i // 16, i % 16
        gate_jb = jb in (2, 5)
        wb = _wmod_buf(i, gate_jb)
        wb_ap = (wb[:] if wb is not None
                 else wslice[:, (i % 4 - 2) * 2048:(i % 4 - 1) * 2048])
        if i >= 4:
            ctx.w(g, "PE", pe_tile[i - 4])
        wm_dma[i] = (ctx.dma_g(wb_ap, wmod[ck * 128:(ck + 1) * 128,
                                           jb * 2048:(jb + 1) * 2048]),
                     wb_ap)

    for i in range(4):
        _emit_wmod_dma(i)
    for jb in range(6):
        gate_jb = jb in (2, 5)
        for ck in range(16):
            i = jb * 16 + ck
            lhs_c = silu_cf if gate_jb else silu_c
            tgt, wb_ap = wm_dma[i]
            ctx.w(te, "LG", tgt)
            for q4 in range(4):
                mm_ = te.matmul(psum[q4][0:1, :], lhs_c[:, ck:ck + 1],
                                wb_ap[:, q4 * 512:(q4 + 1) * 512],
                                start=(ck == 0), stop=(ck == 15))
            pe_tile[i] = ctx.inc(mm_, "PE")
            if i + 4 < 96:
                _emit_wmod_dma(i + 4)
        ctx.w(v, "PE", pe_tile[jb * 16 + 15])
        for q4 in range(4):
            off = jb * 2048 + q4 * 512
            dv_mod = ctx.inc(v.tensor_copy(out=mrow[:, off:off + 512],
                                           in_=psum[q4][0:1, :]), "DV")
        ctx.w(te, "DV", dv_mod)

    SA, CA, GA, SB, CB, GB = (i * C for i in range(6))
    v.tensor_scalar(out=mrow[:, CA:CA + C], in0=mrow[:, CA:CA + C],
                    scalar1=1.0, scalar2=None, op0=ALU.add)
    v.tensor_scalar(out=mrow[:, CB:CB + C], in0=mrow[:, CB:CB + C],
                    scalar1=1.0, scalar2=None, op0=ALU.add)
    for off in (GA, GB):
        sl = mrow[:, off:off + C]
        v.tensor_scalar(out=uf[0:1, 0:C], in0=sl, scalar1=-1.0,
                        scalar2=0.25, op0=ALU.mult, op1=ALU.max)
        v.scalar_tensor_tensor(out=uf[0:1, 0:C], in0=sl, scalar=0.25,
                               in1=uf[0:1, 0:C], op0=ALU.max, op1=ALU.max)
        v.tensor_scalar(out=x_t[0][0:1, 0:C], in0=sl, scalar1=0.0,
                        scalar2=None, op0=ALU.is_ge)
        v.tensor_scalar(out=x_t[0][0:1, 0:C], in0=x_t[0][0:1, 0:C],
                        scalar1=2.0, scalar2=-1.0, op0=ALU.mult, op1=ALU.add)
        v.tensor_tensor(out=sl, in0=uf[0:1, 0:C], in1=x_t[0][0:1, 0:C],
                        op=ALU.mult)
    v.tensor_copy(out=mbf[:, 0:C], in_=mrow[:, CB:CB + C])
    v.tensor_copy(out=mbf[:, C:2 * C], in_=mrow[:, SB:SB + C])
    v.tensor_copy(out=mbf[:, 2 * C:3 * C], in_=mrow[:, GA:GA + C])
    dv_m = ctx.inc(v.tensor_copy(out=mbf[:, 3 * C:4 * C],
                                 in_=mrow[:, GB:GB + C]), "DV")
    t = tap("mod", [1, 6 * C])
    if t is not None:
        ctx.w(sy, "DV", dv_m)
        ctx.dma_s(t, mrow[:])

    def bcast_row(dst, src_row, ones):
        for ocn in range(4):
            ctx.w(te, "DV")
            mm_ = te.matmul(psum[4 + ocn % 2][:], ones[:],
                            src_row[:, ocn * 512:(ocn + 1) * 512],
                            start=True, stop=True)
            pe_b = ctx.inc(mm_, "PE")
            ctx.w(v, "PE", pe_b)
            ctx.inc(v.tensor_copy(out=dst[:, ocn * 512:(ocn + 1) * 512],
                                  in_=psum[4 + ocn % 2][:]), "DV")

    t = tap("mbft", [1, 4 * C], BF16)
    if t is not None:
        ctx.w(sy, "DV", dv_m)
        ctx.dma_s(t, mbf[:])
    ctx.w(te, "DV", dv_m)
    bcast_row(bcA1, mrow[:, CA:CA + C], ones_row_f)
    bcast_row(bcB1, mrow[:, SA:SA + C], ones_row_f)

    if _stop("mod"):
        return nc, taps

    def emit_ln_apply(src_tile, scale_b, shift_b, out_bf):
        vf(v.reduce_sum(col["sum"][:, 0:1], src_tile, axis=AX.X))
        vf(v.tensor_scalar(out=col["mean"][:, 0:1], in0=col["sum"][:, 0:1],
                           scalar1=1.0 / 2048.0, scalar2=None, op0=ALU.mult))
        vf(v.scalar_tensor_tensor(out=trash_bf[:], in0=src_tile,
                                  scalar=col["mean"][:, 0:1], in1=src_tile,
                                  op0=ALU.subtract, op1=ALU.mult,
                                  accum_out=col["var"][:, 0:1]))
        vf(v.tensor_scalar(out=col["var"][:, 0:1], in0=col["var"][:, 0:1],
                           scalar1=1.0 / 2048.0, scalar2=LN_EPS,
                           op0=ALU.mult, op1=ALU.add))
        dvs = ctx.inc(v.reciprocal(out=col["var"][:, 1:2],
                                   in_=col["var"][:, 0:1]), "DV")
        ctx.w(sc, "DV", dvs)
        acs = ctx.inc(sc.activation(out=col["invs"][:, 0:1],
                                    in_=col["var"][:, 1:2], func=AF.Sqrt),
                      "AC")
        ctx.w(v, "AC", acs)
        v.tensor_scalar(out=uf[:], in0=src_tile, scalar1=col["mean"][:, 0:1],
                        scalar2=col["invs"][:, 0:1], op0=ALU.subtract,
                        op1=ALU.mult)
        v.tensor_tensor(out=uf[:], in0=uf[:], in1=scale_b[:], op=ALU.mult)
        return ctx.inc(v.tensor_tensor(out=out_bf, in0=uf[:], in1=shift_b[:],
                                       op=ALU.add), "DV")

    def emit_transpose16(src_bf, dst_fn, dst_col):
        last_dv = 0
        for cg in range(2):
            ctx.w(te, "DV")
            for c8 in range(8):
                cch = cg * 8 + c8
                mm_ = te.transpose(ptr[:, c8 * 128:(c8 + 1) * 128],
                                   src_bf[:, cch * 128:(cch + 1) * 128],
                                   ident[:])
            pe_t = ctx.inc(mm_, "PE")
            ctx.w(v, "PE", pe_t)
            for c8 in range(8):
                cch = cg * 8 + c8
                last = v.tensor_copy(
                    out=dst_fn(cch)[:, dst_col:dst_col + 128],
                    in_=ptr[:, c8 * 128:(c8 + 1) * 128])
            last_dv = ctx.inc(last, "DV")
        return last_dv

    dv_h1 = {}
    ctx.w(sy, "PE")
    ctx.w(sy, "DV")
    for s in range(NT):
        xt = x_t[s % 2]
        if s >= 2:
            ctx.w(sy, "DV", dv_h1[s - 2])
        tgt = ctx.dma_s(xt[:], xb[s * 128:(s + 1) * 128, :])
        ctx.w(v, "LS", tgt)
        dvln = emit_ln_apply(xt[:], bcA1, bcB1, trash_bf[:])
        if s == 0:
            t0 = tap("h1tm", [128, 2048], BF16)
            if t0 is not None:
                ctx.w(sy, "DV", dvln)
                ctx.dma_s(t0, trash_bf[:])
            t0 = tap("x0", [128, 2048])
            if t0 is not None:
                ctx.w(sy, "DV", dvln)
                ctx.dma_s(t0, xt[:])
        dv_h1[s] = emit_transpose16(trash_bf, lambda cch: h1_fm[cch],
                                    s * 128)

    t = tap("h1fm", [128, 2048 * 16], BF16)
    if t is not None:
        ctx.w(sy, "DV", dv_h1[NT - 1])
        for cch in range(16):
            ctx.dma_s(t[:, cch * 2048:(cch + 1) * 2048], h1_fm[cch][:])

    if _stop("h1"):
        return nc, taps

    gamma_pieces = []
    for i in range(16):
        for p in range(4):
            gamma_pieces.append((gwT[i * 128:(i + 1) * 128,
                                     p * 2048:(p + 1) * 2048], i * 4 + p))
    for i in range(16):
        for p in range(4):
            gamma_pieces.append((vwT[i * 128:(i + 1) * 128,
                                     p * 2048:(p + 1) * 2048],
                                 64 + i * 4 + p))
    for i in range(64):
        gamma_pieces.append((owT[i * 128:(i + 1) * 128, :], 128 + i))
    gamma_ac = {}
    gamma_k = [0]

    def emit_gamma(n):
        for _ in range(n):
            k = gamma_k[0]
            if k >= len(gamma_pieces):
                return
            gamma_k[0] = k + 1
            src, gcol = gamma_pieces[k]
            wb = gstr[k % 2]
            if k >= 2:
                ctx.w(g, "AC", gamma_ac[k - 2])
            else:
                ctx.w(g, "PE")
            tgt = ctx.dma_g(wb[:], src)
            ctx.w(sc, "LG", tgt)
            gamma_ac[k] = ctx.inc(
                sc.activation(out=trash_bf[:], in_=wb[:], func=AF.Abs,
                              accum_out=gabs[:, gcol:gcol + 1]), "AC")

    SECQ, SECK, SECV = 0, 2048, 4096

    proj_bank = [0, 0]
    for p2 in range(8):
        heads = [p2 * 2, p2 * 2 + 1]
        for sec_i, (sec_off, tiles) in enumerate(
                ((SECQ, range(8, 16)), (SECK, range(16)), (SECV, range(16)))):
            csel = sec_off + p2 * 256
            w3 = wqkvT[:, csel:csel + 256].rearrange("(k p) n -> p k n",
                                                     p=128)
            ctx.w(g, "PE")
            ctx.w(g, "DV")
            tgt = ctx.dma_g(wslice[:], w3)
            ctx.w(te, "LG", tgt)
            for si, s in enumerate(tiles):
                pb = psum[si % 2]
                ctx.w(te, "DV", proj_bank[si % 2])
                for cch in range(16):
                    mm_ = te.matmul(pb[:, 0:256],
                                    h1_fm[cch][:, s * 128:(s + 1) * 128],
                                    wslice[:, cch * 256:(cch + 1) * 256],
                                    start=(cch == 0), stop=(cch == 15))
                pe_mm = ctx.inc(mm_, "PE")
                ctx.w(v, "PE", pe_mm)
                if sec_i == 0:
                    dst_ap = q_tm[:, (s - 8) * 256:(s - 7) * 256]
                elif sec_i == 1:
                    dst_ap = k_tm[:, s * 256:(s + 1) * 256]
                else:
                    dst_ap = v_tm[:, s * 256:(s + 1) * 256]
                proj_bank[si % 2] = ctx.inc(
                    v.tensor_copy(out=dst_ap, in_=pb[:, 0:256]), "DV")

        def rope_rms(buf, s, tile_idx):
            base = tile_idx * 256
            so = s * 64
            sin_s = sin_tm[:, so:so + 64]
            cos_s = cos_tm[:, so:so + 64]
            for hl in range(2):
                lo = buf[:, base + hl * 128: base + hl * 128 + 64]
                hi = buf[:, base + hl * 128 + 64: base + hl * 128 + 128]
                v.tensor_tensor(out=uf[:, 0:64], in0=lo, in1=cos_s,
                                op=ALU.mult)
                v.tensor_tensor(out=uf[:, 64:128], in0=hi, in1=sin_s,
                                op=ALU.mult)
                v.tensor_tensor(out=uf[:, 128:192], in0=hi, in1=cos_s,
                                op=ALU.mult)
                v.tensor_tensor(out=uf[:, 192:256], in0=lo, in1=sin_s,
                                op=ALU.mult)
                v.tensor_tensor(out=lo, in0=uf[:, 0:64], in1=uf[:, 64:128],
                                op=ALU.subtract)
                v.tensor_tensor(out=hi, in0=uf[:, 128:192],
                                in1=uf[:, 192:256], op=ALU.add)
                full = buf[:, base + hl * 128: base + hl * 128 + 128]
                ii = v.scalar_tensor_tensor(
                    out=trash_bf[:, 0:128], in0=full,
                    scalar=1.0, in1=full, op0=ALU.mult, op1=ALU.mult,
                    accum_out=rms_c[:, hl:hl + 1])
            vf(ii)
            vf(v.tensor_scalar(out=rms_c[:, 0:2], in0=rms_c[:, 0:2],
                               scalar1=1.0 / 128.0, scalar2=RMS_EPS,
                               op0=ALU.mult, op1=ALU.add))
            dvs = ctx.inc(v.reciprocal(out=rms_c[:, 2:4],
                                       in_=rms_c[:, 0:2]), "DV")
            ctx.w(sc, "DV", dvs)
            acs = ctx.inc(sc.activation(out=rms_c[:, 4:6], in_=rms_c[:, 2:4],
                                        func=AF.Sqrt), "AC")
            ctx.w(v, "AC", acs)
            for hl in range(2):
                sl = buf[:, base + hl * 128: base + hl * 128 + 128]
                v.tensor_scalar(out=sl, in0=sl,
                                scalar1=rms_c[:, 4 + hl:5 + hl],
                                scalar2=None, op0=ALU.mult)
            return ctx.inc(v.engine_nop(), "DV")

        if p2 == 0:
            t0 = tap("qkv0", [128, 2048 * 3], BF16)
            if t0 is not None:
                ctx.w(sy, "DV")
                ctx.dma_s(t0[:, 0:2048], q_tm[:])
                ctx.dma_s(t0[:, 2048:2048 + 4096], k_tm[:])
                ctx.dma_s(t0[:, 6144 - 2048:6144], v_tm[:, 0:2048])

        for s in range(8, 16):
            rope_rms(q_tm, s, s - 8)
        dv_rope = 0
        for s in range(16):
            dv_rope = rope_rms(k_tm, s, s)

        if p2 == 0:
            t0 = tap("rope0", [128, 2048 * 3], BF16)
            if t0 is not None:
                ctx.w(sy, "DV")
                ctx.dma_s(t0[:, 0:2048], q_tm[:])
                ctx.dma_s(t0[:, 2048:6144], k_tm[:])

        for hl in range(2):
            hglob = heads[hl]
            jobs = [(q_tm, s * 256 + hl * 128, q_fm[hl], s * 128)
                    for s in range(8)]
            jobs += [(k_tm, s * 256 + hl * 128, k_fm[hl], s * 128)
                     for s in range(16)]
            ctx.w(te, "DV", dv_rope)
            for j0 in range(0, len(jobs), 8):
                batch = jobs[j0:j0 + 8]
                ctx.w(te, "DV")
                for bi, (sb_, soff, dbuf, doff) in enumerate(batch):
                    mm_ = te.transpose(ptr[:, bi * 128:(bi + 1) * 128],
                                       sb_[:, soff:soff + 128], ident[:])
                pe_t = ctx.inc(mm_, "PE")
                ctx.w(v, "PE", pe_t)
                for bi, (sb_, soff, dbuf, doff) in enumerate(batch):
                    last = v.tensor_copy(out=dbuf[:, doff:doff + 128],
                                         in_=ptr[:, bi * 128:(bi + 1) * 128])
                ctx.inc(last, "DV")
            dv_tp = ctx.cnt["DV"]
            ctx.w(te, "DV", dv_tp)

            for qb in range(2):
                nck = 12 if qb == 0 else 16
                ac_exp = {}
                pe_use = {}
                pe_sc = {}
                ctx.w(sc, "PE")
                ctx.w(te, "AC")

                def emit_scores(c_, qb=qb, hl=hl):
                    ps = psum[2 + c_ % 2]
                    if c_ >= 2:
                        ctx.w(te, "AC", ac_exp[c_ - 2])
                    mm_ = te.matmul(ps[:],
                                    k_fm[hl][:, c_ * 128:(c_ + 1) * 128],
                                    q_fm[hl][:, qb * 512:(qb + 1) * 512],
                                    start=True, stop=True)
                    pe_sc[c_] = ctx.inc(mm_, "PE")

                emit_scores(0)
                for c_ in range(nck):
                    v_sl = v_tm[:, c_ * 256 + hl * 128:
                                c_ * 256 + hl * 128 + 128]
                    ps = psum[2 + c_ % 2]
                    eb = exp_sb[c_ % 2]
                    pe_s = pe_sc[c_]
                    ctx.w(sc, "PE", pe_s)
                    if c_ >= 2:
                        ctx.w(sc, "PE", pe_use[c_ - 2])
                    ci = CIDX[(qb, c_)]
                    if p2 == 0 and hl == 0 and qb == 1 and c_ == 0:
                        t0 = tap("sc0", [128, 512])
                        if t0 is not None:
                            ctx.w(v, "PE", pe_s)
                            dtap = ctx.inc(v.tensor_copy(
                                out=uf[:, 0:512], in_=ps[:]), "DV")
                            ctx.w(sy, "DV", dtap)
                            ctx.dma_s(t0, uf[:, 0:512])
                            ctx.w(sc, "LS")
                    ac_exp[c_] = ctx.inc(
                        sc.activation(out=eb[:], in_=ps[:], func=AF.Exp,
                                      scale=col["sc"][:, 0:1],
                                      bias=btab[:, ci:ci + 1]), "AC")
                    if p2 == 0 and hl == 0 and qb == 1 and c_ == 0:
                        t0 = tap("exp0", [128, 512], BF16)
                        if t0 is not None:
                            ctx.w(sy, "AC", ac_exp[c_])
                            ctx.dma_s(t0, eb[:])
                            ctx.w(te, "LS")
                    if c_ + 1 < nck:
                        emit_scores(c_ + 1)
                    Dd = c_ - 8 - 4 * qb
                    if 0 <= Dd <= 3:
                        w_ = (Dd + 1) * 128
                        ctx.w(v, "AC", ac_exp[c_])
                        dv_msk = ctx.inc(v.tensor_tensor(
                            out=eb[:, 0:w_], in0=eb[:, 0:w_],
                            in1=dm3[:, (3 - Dd) * 128:512], op=ALU.mult),
                            "DV")
                        ctx.w(te, "DV", dv_msk)
                    else:
                        ctx.w(te, "AC", ac_exp[c_])
                    st_, sp_ = (c_ == 0), (c_ == nck - 1)
                    te.matmul(psum[4][:], v_sl, eb[:], start=st_, stop=sp_)
                    mm_ = te.matmul(prow[:], ones_col_bf[:], eb[:],
                                    start=st_, stop=sp_)
                    pe_use[c_] = ctx.inc(mm_, "PE")
                ctx.w(v, "PE", pe_use[nck - 1])
                if p2 == 0 and hl == 0 and qb == 1:
                    t0 = tap("pat0", [128, 512])
                    if t0 is not None:
                        dtap = ctx.inc(v.tensor_copy(
                            out=uf[:, 0:512], in_=psum[4][:]), "DV")
                        ctx.w(sy, "DV", dtap)
                        ctx.dma_s(t0, uf[:, 0:512])
                    t0 = tap("prow0", [1, 512])
                    if t0 is not None:
                        dtap = ctx.inc(v.tensor_copy(
                            out=uf[0:1, 512:1024], in_=prow[:]), "DV")
                        ctx.w(sy, "DV", dtap)
                        tgt_t = ctx.dma_s(t0, uf[0:1, 512:1024])
                        ctx.w(v, "LS", tgt_t)
                dv_i = ctx.inc(v.reciprocal(out=inv_sb[:], in_=prow[:]),
                               "DV")
                ctx.w(te, "DV", dv_i)
                pe_b = ctx.inc(te.matmul(psum[5][:], ones_row_f[:],
                                         inv_sb[:], start=True, stop=True),
                               "PE")
                ctx.w(v, "PE", pe_b)
                ctx.inc(v.tensor_copy(out=inv_b[:], in_=psum[5][:]), "DV")
                dv_at = ctx.inc(v.tensor_tensor(
                    out=attn_fm[hglob][:, qb * 512:(qb + 1) * 512],
                    in0=psum[4][:], in1=inv_b[:], op=ALU.mult), "DV")
                ctx.w(te, "DV", dv_at)

        emit_gamma(24)

    t = tap("attnfm", [128, 1024 * 16], BF16)
    if t is not None:
        ctx.w(sy, "DV")
        ctx.w(sy, "AC")
        for i in range(16):
            ctx.dma_s(t[:, i * 1024:(i + 1) * 1024], attn_fm[i][:])

    if _stop("attn"):
        return nc, taps

    emit_gamma(len(gamma_pieces))
    ctx.w(v, "AC")
    v.reduce_sum(col["gmax"][:, 0:1], gabs[:, 0:64], axis=AX.X)
    v.reduce_sum(col["gmax"][:, 1:2], gabs[:, 64:128], axis=AX.X)
    dv_g = ctx.inc(v.reduce_sum(col["tmp"][:, 0:1], gabs[:, 128:192],
                                axis=AX.X), "DV")
    ctx.w(te, "DV", dv_g)
    te.matmul(prow[:, 0:2], ones_col_f[:], col["gmax"][:, 0:2],
              start=True, stop=True)
    pe_g = ctx.inc(te.matmul(prow[:, 2:3], ones_col_f[:], col["tmp"][:, 0:1],
                             start=True, stop=True), "PE")
    ctx.w(v, "PE", pe_g)
    vf(v.tensor_copy(out=grow[:, 0:3], in_=prow[:, 0:3]))
    for i in range(3):
        vf(v.tensor_scalar(out=grow[:, i:i + 1], in0=grow[:, i:i + 1],
                           scalar1=1.0 / (C * HID), scalar2=1e-5,
                           op0=ALU.mult, op1=ALU.max))
    for i in range(3):
        v.tensor_scalar(out=grow[:, 4 + 2 * i:5 + 2 * i],
                        in0=grow[:, i:i + 1], scalar1=0.5, scalar2=None,
                        op0=ALU.mult)
        vf(v.tensor_scalar(out=grow[:, 5 + 2 * i:6 + 2 * i],
                           in0=grow[:, i:i + 1], scalar1=-0.5, scalar2=None,
                           op0=ALU.mult))
    dv_g2 = ctx.inc(v.reciprocal(out=grow[:, 10:13], in_=grow[:, 0:3]), "DV")
    ctx.w(v, "DV", dv_g2)
    ctx.w(te, "DV", dv_g2)
    te.matmul(psum[0][:, 0:6], ones_row_f[:], grow[:, 4:10],
              start=True, stop=True)
    pe_g2 = ctx.inc(te.matmul(psum[0][:, 6:8], ones_row_f[:], grow[:, 0:2],
                              start=True, stop=True), "PE")
    ctx.w(v, "PE", pe_g2)
    dv_g3 = ctx.inc(v.tensor_copy(out=gam[:], in_=psum[0][:, 0:8]), "DV")
    t = tap("gamma", [1, 16])
    if t is not None:
        ctx.w(sy, "DV", dv_g3)
        ctx.dma_s(t, grow[:])

    if _stop("gamma"):
        return nc, taps

    t = tap("mbf4", [1, 4 * C], BF16)
    if t is not None:
        ctx.w(sy, "DV")
        ctx.w(sy, "AC")
        ctx.dma_s(t, mbf[:])
    bcast_row(bc2, mbf[:, 2 * C:3 * C], ones_row_bf)
    ctx.w(g, "PE")
    lg_wo = 0
    for cch in range(16):
        lg_wo = ctx.dma_g(wo_sb[cch][:], woT[cch * 128:(cch + 1) * 128, :])
    wo_bank = [ctx.cnt["DV"], ctx.cnt["DV"]]
    ctx.w(sy, "PE")
    ctx.w(sy, "AC")
    dv_ep = {}
    spill_ls = {}
    for s in range(8):
        xn = xnew_t[s % 2]
        if s >= 2:
            ctx.w(sy, "DV", dv_ep[s - 2])
        if s >= 1:
            ctx.w(sy, "DV", dv_ep[s - 1])
        tgt_o = ctx.dma_s(xo_t[:], xb[(8 + s) * 128:(9 + s) * 128, :])
        ctx.w(te, "LG", lg_wo)
        for oc in range(4):
            pb = psum[oc % 2]
            ctx.w(te, "DV", wo_bank[oc % 2])
            if s >= 2 and oc == 0:
                ctx.w(v, "LS", spill_ls[s - 2])
            for cch in range(16):
                mm_ = te.matmul(pb[:],
                                attn_fm[cch][:, s * 128:(s + 1) * 128],
                                wo_sb[cch][:, oc * 512:(oc + 1) * 512],
                                start=(cch == 0), stop=(cch == 15))
            pe_mm = ctx.inc(mm_, "PE")
            ctx.w(v, "PE", pe_mm)
            if s == 0 and oc == 0:
                t0 = tap("wops", [128, 512])
                if t0 is not None:
                    dtp = ctx.inc(v.tensor_copy(out=ef[:], in_=pb[:]), "DV")
                    ctx.w(sy, "DV", dtp)
                    ctx.dma_s(t0, ef[:])
                t0 = tap("bc2t", [128, 2048])
                if t0 is not None:
                    ctx.dma_s(t0, bc2[:])
            wo_bank[oc % 2] = ctx.inc(v.tensor_tensor(
                out=xn[:, oc * 512:(oc + 1) * 512], in0=pb[:],
                in1=bc2[:, oc * 512:(oc + 1) * 512], op=ALU.mult), "DV")
        ctx.w(v, "LS", tgt_o)
        dv_x = ctx.inc(v.tensor_tensor(out=xn[:], in0=xn[:], in1=xo_t[:],
                                       op=ALU.add), "DV")
        dv_ep[s] = dv_x
        ctx.w(sy, "DV", dv_x)
        spill_ls[s] = ctx.dma_s(xnew_hbm[s * 128:(s + 1) * 128, :], xn[:])

    t = tap("xnew", [1024, C])
    if t is not None:
        ctx.w(sy, "LS", ctx.cnt["LS"])
        ctx.dma_s(t, xnew_hbm)

    if _stop("xnew"):
        return nc, taps

    bcast_row(bcA2, mbf[:, 0:C], ones_row_bf)
    bcast_row(bcB2, mbf[:, C:2 * C], ones_row_bf)
    ctx.w(sy, "LS", ctx.cnt["LS"])
    dv_q2 = {}
    for s in range(8):
        xn = xnew_t[s % 2]
        if s >= 2:
            ctx.w(sy, "DV", dv_q2[s - 2])
        tgt = ctx.dma_s(xn[:], xnew_hbm[s * 128:(s + 1) * 128, :])
        ctx.w(v, "LS", tgt)
        emit_ln_apply(xn[:], bcA2, bcB2, trash_bf[:])
        v.tensor_tensor(out=uf[:], in0=uf[:], in1=bcB2[:], op=ALU.add)
        vf(v.reduce_max(col["gmax"][:, 0:1], uf[:], axis=AX.X,
                        apply_absolute_value=True))
        vf(v.tensor_scalar(out=col["gmax"][:, 0:1], in0=col["gmax"][:, 0:1],
                           scalar1=1e-5, scalar2=None, op0=ALU.max))
        vf(v.reciprocal(out=col["srow"][:, 0:1], in_=col["gmax"][:, 0:1]))
        vf(v.tensor_scalar(out=col["srow"][:, 0:1], in0=col["srow"][:, 0:1],
                           scalar1=127.0, scalar2=None, op0=ALU.mult))
        vf(v.tensor_scalar(out=col["invsr"][:, 0:1], in0=col["gmax"][:, 0:1],
                           scalar1=1.0 / 127.0, scalar2=None, op0=ALU.mult))
        v.tensor_scalar(out=uf[:], in0=uf[:], scalar1=col["srow"][:, 0:1],
                        scalar2=128.0, op0=ALU.mult, op1=ALU.add)
        for ch in range(4):
            usl = uf[:, ch * 512:(ch + 1) * 512]
            v.tensor_copy(out=u32[:], in_=usl)
            v.tensor_tensor(out=ef[:], in0=usl, in1=u32[:], op=ALU.subtract)
            v.scalar_tensor_tensor(out=ef[:], in0=ef[:],
                                   scalar=col["half"][:, 0:1], in1=u32[:],
                                   op0=ALU.is_ge, op1=ALU.add)
            v.tensor_scalar(out=trash_bf[:, ch * 512:(ch + 1) * 512],
                            in0=ef[:], scalar1=-128.0,
                            scalar2=col["invsr"][:, 0:1],
                            op0=ALU.add, op1=ALU.mult)
        ctx.inc(v.engine_nop(), "DV")
        dv_q2[s] = emit_transpose16(trash_bf, lambda cch: q_h2[cch],
                                    s * 128)

    t = tap("qh2", [128, 1024 * 16], BF16)
    if t is not None:
        ctx.w(sy, "DV")
        for i in range(16):
            ctx.dma_s(t[:, i * 1024:(i + 1) * 1024], q_h2[i][:])

    if _stop("qh2"):
        return nc, taps

    ctx.w(g, "PE")
    ctx.w(g, "DV")
    ctx.dma_g(brow_g[:], gb_in)
    lg_b = ctx.dma_g(brow_v[:], vb_in)
    ctx.w(v, "LG", lg_b)
    ctx.w(v, "DV", dv_g2)
    v.tensor_scalar(out=brow_g[:], in0=brow_g[:], scalar1=grow[:, 10:11],
                    scalar2=None, op0=ALU.mult)
    v.tensor_scalar(out=brow_v[:], in0=brow_v[:], scalar1=grow[:, 11:12],
                    scalar2=None, op0=ALU.mult)
    ctx.inc(v.tensor_scalar(out=brow_o[:], in0=brow_o[:],
                            scalar1=grow[:, 12:13], scalar2=None,
                            op0=ALU.mult), "DV")

    def emit_ternary(dst_ap, src_ap, cpos, cneg):
        v.tensor_scalar(out=wq_a[:], in0=src_ap,
                        scalar1=gam[:, cpos:cpos + 1], scalar2=None,
                        op0=ALU.is_ge)
        v.tensor_scalar(out=wq_b[:], in0=src_ap,
                        scalar1=gam[:, cneg:cneg + 1], scalar2=None,
                        op0=ALU.is_le)
        return v.tensor_tensor(out=dst_ap, in0=wq_a[:], in1=wq_b[:],
                               op=ALU.subtract)

    ctx.w(v, "PE")
    pe_gv = {}
    spill6 = {}
    regs6 = [(gw_bf, vw_bf), (wg_op, wv_op)]
    lgp6 = {}
    dvq6 = {}
    acg_p = {0: ctx.cnt["AC"], 1: ctx.cnt["AC"]}
    stt_p = {0: ctx.cnt["DV"], 1: ctx.cnt["DV"]}

    def dma6(h):
        Rg_, Rv_ = regs6[h % 2]
        if h >= 2:
            ctx.w(g, "PE", pe_gv[h - 2])
        lgp6[h] = []
        for wsrc, Rd in ((gwT, Rg_), (vwT, Rv_)):
            for p in range(4):
                src = wsrc[p * 512:(p + 1) * 512,
                           h * 512:(h + 1) * 512].rearrange(
                    "(k p) n -> p k n", p=128)
                lgp6[h].append(
                    ctx.dma_g(Rd[:, p * 2048:(p + 1) * 2048], src))

    def q6_chunk(h, c):
        Rg_, Rv_ = regs6[h % 2]
        ctx.w(v, "LG", lgp6[h][c // 4])
        if c < 16:
            src = Rg_[:, c * 512:(c + 1) * 512]
            return emit_ternary(src, src, 0, 1)
        src = Rv_[:, (c - 16) * 512:(c - 15) * 512]
        return emit_ternary(src, src, 2, 3)

    dma6(0)
    for c in range(32):
        last = q6_chunk(0, c)
    dvq6[0] = ctx.inc(last, "DV")
    for hc in range(16):
        Rg, Rv = regs6[hc % 2]
        if hc + 1 < 16:
            dma6(hc + 1)
        ctx.w(te, "DV", dvq6[hc])
        for s in range(8):
            pg = psum[0 if s % 2 == 0 else 2]
            pv = psum[1 if s % 2 == 0 else 3]
            gsP = gate_s if s % 2 == 0 else gate_s2
            ctx.w(te, "AC", acg_p[s % 2])
            for cch in range(16):
                te.matmul(pg[:], q_h2[cch][:, s * 128:(s + 1) * 128],
                          Rg[:, cch * 512:(cch + 1) * 512],
                          start=(cch == 0), stop=False)
            mm_ = te.matmul(pg[:], ones_row_bf[:],
                            brow_g[:, hc * 512:(hc + 1) * 512],
                            start=False, stop=True)
            pe_gm = ctx.inc(mm_, "PE")
            ctx.w(sc, "PE", pe_gm)
            ctx.w(sc, "DV", stt_p[s % 2])
            acg = ctx.inc(sc.activation(out=gsP[:], in_=pg[:],
                                        func=AF.Silu, scale=gam[:, 6:7]),
                          "AC")
            acg_p[s % 2] = acg
            ctx.w(te, "DV", stt_p[s % 2])
            for cch in range(16):
                te.matmul(pv[:], q_h2[cch][:, s * 128:(s + 1) * 128],
                          Rv[:, cch * 512:(cch + 1) * 512],
                          start=(cch == 0), stop=False)
            mm_ = te.matmul(pv[:], ones_row_bf[:],
                            brow_v[:, hc * 512:(hc + 1) * 512],
                            start=False, stop=True)
            pe_vm = ctx.inc(mm_, "PE")
            pe_gv[hc] = pe_vm
            if hc + 1 < 16:
                for c in range(s * 4, s * 4 + 4):
                    last = q6_chunk(hc + 1, c)
                if s == 7:
                    dvq6[hc + 1] = ctx.inc(last, "DV")
            ctx.w(v, "PE", pe_vm)
            ctx.w(v, "AC", acg)
            it6 = hc * 8 + s
            gvb = gv_s if it6 % 2 == 0 else ef
            if it6 >= 2:
                ctx.w(v, "LS", spill6.get(it6 - 2, 0))
            stt_p[s % 2] = ctx.inc(
                v.scalar_tensor_tensor(out=gvb[:], in0=pv[:],
                                       scalar=gam[:, 7:8], in1=gsP[:],
                                       op0=ALU.mult, op1=ALU.mult), "DV")
            if hc == 0:
                dvm = ctx.inc(v.reduce_max(gvmax[:, s:s + 1], gvb[:],
                                           axis=AX.X,
                                           apply_absolute_value=True), "DV")
            else:
                vf(v.reduce_max(col["tmp"][:, 0:1], gvb[:], axis=AX.X,
                                apply_absolute_value=True))
                dvm = ctx.inc(v.tensor_tensor(out=gvmax[:, s:s + 1],
                                              in0=gvmax[:, s:s + 1],
                                              in1=col["tmp"][:, 0:1],
                                              op=ALU.max), "DV")
            ctx.w(sy, "DV", dvm)
            spill6[it6] = ctx.dma_s(gv_hbm[s * 128:(s + 1) * 128,
                                           hc * 512:(hc + 1) * 512], gvb[:])

    t = tap("gv", [1024, HID])
    if t is not None:
        ctx.w(sy, "LS", ctx.cnt["LS"])
        ctx.dma_s(t, gv_hbm)

    ctx.w(v, "LS", ctx.cnt["LS"])
    vf(v.tensor_scalar(out=gvmax[:], in0=gvmax[:], scalar1=1e-5,
                       scalar2=None, op0=ALU.max))
    vf(v.reciprocal(out=s2c[:], in_=gvmax[:]))
    vf(v.tensor_scalar(out=s2c[:], in0=s2c[:], scalar1=127.0, scalar2=None,
                       op0=ALU.mult))
    vf(v.tensor_scalar(out=is2c[:], in0=gvmax[:],
                       scalar1=1.0 / 127.0, scalar2=None, op0=ALU.mult))

    if _stop("gv"):
        return nc, taps

    bcast_row(bc2, mbf[:, 3 * C:4 * C], ones_row_bf)
    vf(v.tensor_scalar(out=col["tmp"][:, 0:1], in0=gam[:, 4:5],
                       scalar1=2.0, scalar2=None, op0=ALU.mult))
    dv_bc2 = ctx.inc(v.tensor_scalar(out=bc2[:], in0=bc2[:],
                                     scalar1=col["tmp"][:, 0:1],
                                     scalar2=None, op0=ALU.mult), "DV")

    ctx.w(te, "AC")
    acc_bank = [ctx.cnt["DV"]] * 4
    gstr_dv = [ctx.cnt["DV"]] * 2
    for hq in range(4):
        dv_x2 = {}
        dv_q0 = ctx.cnt["DV"]
        ctx.w(sy, "DV")
        for s in range(8):
            for hcl in range(4):
                i = s * 4 + hcl
                xqb = xq[i % 2]
                if i >= 2:
                    ctx.w(sy, "DV", dv_x2.get(i - 2, 0))
                tgt = ctx.dma_s(xqb[:],
                                gv_hbm[s * 128:(s + 1) * 128,
                                       hq * 2048 + hcl * 512:
                                       hq * 2048 + (hcl + 1) * 512])
                ctx.w(v, "LS", tgt)
                v.tensor_scalar(out=uf[:, 0:512], in0=xqb[:],
                                scalar1=s2c[:, s:s + 1], scalar2=128.0,
                                op0=ALU.mult, op1=ALU.add)
                v.tensor_copy(out=u32[:], in_=uf[:, 0:512])
                v.tensor_tensor(out=ef[:], in0=uf[:, 0:512], in1=u32[:],
                                op=ALU.subtract)
                v.scalar_tensor_tensor(out=ef[:], in0=ef[:],
                                       scalar=col["half"][:, 0:1],
                                       in1=u32[:], op0=ALU.is_ge,
                                       op1=ALU.add)
                dvx = ctx.inc(v.tensor_scalar(out=uf[:, 512:1024],
                                              in0=ef[:], scalar1=-128.0,
                                              scalar2=is2c[:, s:s + 1],
                                              op0=ALU.add, op1=ALU.mult),
                              "DV")
                ctx.w(te, "DV", dvx)
                ctx.w(te, "DV", dv_x2.get(i - 2, dv_q0))
                ptb = psum[4 + i % 2]
                for q4 in range(4):
                    mm_ = te.transpose(
                        ptb[:, q4 * 128:(q4 + 1) * 128],
                        uf[:, 512 + q4 * 128:512 + (q4 + 1) * 128],
                        identf[:])
                pe_t = ctx.inc(mm_, "PE")
                ctx.w(v, "PE", pe_t)
                for q4 in range(4):
                    hid128 = hcl * 4 + q4
                    dst = bufA[hid128 // 2]
                    coff = (hid128 % 2) * 1024 + s * 128
                    last = v.tensor_copy(out=dst[:, coff:coff + 128],
                                         in_=ptb[:, q4 * 128:
                                                 (q4 + 1) * 128])
                dv_x2[i] = ctx.inc(last, "DV")

        for oc in range(4):
            dv_wo = 0
            ctx.w(v, "PE")
            for hcl in range(16):
                src = owT[hq * 2048 + hcl * 128: hq * 2048 + (hcl + 1) * 128,
                          oc * 512:(oc + 1) * 512]
                wb = gstr[hcl % 2]
                ctx.w(g, "DV", gstr_dv[hcl % 2])
                lg_o = ctx.dma_g(wb[:, 0:512], src)
                ctx.w(v, "LG", lg_o)
                last = emit_ternary(wq_col[:, hcl * 512:(hcl + 1) * 512],
                                    wb[:, 0:512], 4, 5)
                dv_wo = ctx.inc(last, "DV")
                gstr_dv[hcl % 2] = dv_wo
            ctx.w(te, "DV", dv_wo)
            for tg in range(2):
                for s4 in range(4):
                    s = tg * 4 + s4
                    pb = psum[s4]
                    ctx.w(te, "DV", acc_bank[s4])
                    for hcl in range(16):
                        mm_ = te.matmul(pb[:], lhs_ := bufA[hcl // 2][
                            :, (hcl % 2) * 1024 + s * 128:
                            (hcl % 2) * 1024 + s * 128 + 128],
                            wq_col[:, hcl * 512:(hcl + 1) * 512],
                            start=(hcl == 0),
                            stop=(hq < 3 and hcl == 15))
                    if hq == 3:
                        mm_ = te.matmul(pb[:], ones_row_bf[:],
                                        brow_o[:, oc * 512:(oc + 1) * 512],
                                        start=False, stop=True)
                    pe_o = ctx.inc(mm_, "PE")
                    ctx.w(v, "PE", pe_o)
                    if s == 7 and oc == 0:
                        t0 = tap(f"ps7q{hq}", [128, 512])
                        if t0 is not None:
                            dtp = ctx.inc(v.tensor_copy(out=ef[:],
                                                        in_=pb[:]), "DV")
                            ctx.w(sy, "DV", dtp)
                            ctx.dma_s(t0, ef[:])
                    if hq == 0:
                        dv_a = ctx.inc(v.tensor_copy(
                            out=acc[s][:, oc * 512:(oc + 1) * 512],
                            in_=pb[:]), "DV")
                    else:
                        dv_a = ctx.inc(v.tensor_tensor(
                            out=acc[s][:, oc * 512:(oc + 1) * 512],
                            in0=acc[s][:, oc * 512:(oc + 1) * 512],
                            in1=pb[:], op=ALU.add), "DV")
                    acc_bank[s4] = dv_a

    t = tap("acc0", [128, 2048])
    if t is not None:
        ctx.w(sy, "DV")
        ctx.dma_s(t, acc[0][:])
    t = tap("acc7", [128, 2048])
    if t is not None:
        ctx.w(sy, "DV")
        ctx.dma_s(t, acc[7][:])
    t = tap("x2fm0", [128, 2048], BF16)
    if t is not None:
        ctx.w(sy, "DV")
        ctx.dma_s(t, bufA[0][:])
    dv_y = 0
    ctx.w(sy, "DV")
    for s in range(8):
        if s >= 1:
            ctx.w(sy, "DV", dv_y)
        tgt = ctx.dma_s(xo_t[:], xnew_hbm[s * 128:(s + 1) * 128, :])
        ctx.w(v, "LS", tgt)
        ctx.w(v, "DV", dv_bc2)
        v.tensor_tensor(out=uf[:], in0=acc[s][:], in1=bc2[:], op=ALU.mult)
        dv_y = ctx.inc(v.tensor_tensor(out=uf[:], in0=uf[:], in1=xo_t[:],
                                       op=ALU.add), "DV")
        ctx.w(sy, "DV", dv_y)
        ctx.dma_s(y_out[s * 128:(s + 1) * 128, :], uf[:])
        ctx.w(v, "LS")

    sy.wait_ge(ctx.sem["LS"], ctx.cnt["LS"])
    return nc, taps


def _host_prep(L0_attn_x, L0_attn_W_qkv, L0_attn_W_o, L0_ffn_gate_w,
               L0_ffn_gate_b, L0_ffn_val_w, L0_ffn_val_b, L0_ffn_out_w,
               L0_ffn_out_b, L0_c, L0_wmod):
    import ml_dtypes
    x = np.ascontiguousarray(np.asarray(L0_attn_x, dtype=np.float32))
    wqkv = np.asarray(L0_attn_W_qkv, dtype=np.float32)
    perm = np.empty(128, dtype=np.int64)
    perm[0:64] = np.arange(64) * 2
    perm[64:128] = np.arange(64) * 2 + 1
    rows = np.arange(3 * C)
    sec, rem = rows // C, rows % C
    h_, d_ = rem // D, rem % D
    newsrc = np.where(sec < 2, sec * C + h_ * D + perm[d_], rows)
    wqkvT = np.ascontiguousarray(wqkv[newsrc].T)
    woT = np.ascontiguousarray(np.asarray(L0_attn_W_o, np.float32).T)
    gwT = np.ascontiguousarray(np.asarray(L0_ffn_gate_w, np.float32).T)
    vwT = np.ascontiguousarray(np.asarray(L0_ffn_val_w, np.float32).T)
    owT = np.ascontiguousarray(np.asarray(L0_ffn_out_w, np.float32).T)
    wmod = np.ascontiguousarray(np.asarray(L0_wmod, np.float32))
    carr = np.asarray(L0_c, dtype=np.float32)
    gb = np.ascontiguousarray(np.asarray(L0_ffn_gate_b,
                                         np.float32).reshape(1, HID))
    vb = np.ascontiguousarray(np.asarray(L0_ffn_val_b,
                                         np.float32).reshape(1, HID))
    ob = np.ascontiguousarray(np.asarray(L0_ffn_out_b,
                                         np.float32).reshape(1, C))
    identb = np.eye(128, dtype=ml_dtypes.bfloat16)
    identf = np.eye(128, dtype=np.float32)

    in_maps = []
    for core in range(8):
        b, half = core // 2, core % 2
        if half == 1:
            xb_eff = x[b]
        else:
            xb_eff = np.concatenate([x[b, 1024:2048], x[b, 0:1024]], axis=0)
        bt = np.full((128, NCOMB), EXP_SHIFT, dtype=np.float32)
        if half == 0:
            for j, (qb, c_) in enumerate(COMBOS):
                if c_ < 8:
                    bt[:, j] = DEAD
        pos = np.zeros((128, 16), dtype=np.float32)
        for s in range(16):
            if half == 0:
                base = (s - 8) * 128 if s >= 8 else s * 128
            else:
                base = s * 128
            pos[:, s] = base + np.arange(128)
        ccol_ = np.ascontiguousarray(carr[b].reshape(16, 128).T)
        in_maps.append({
            "xb": np.ascontiguousarray(xb_eff), "wqkvT": wqkvT, "woT": woT,
            "gwT": gwT, "vwT": vwT, "owT": owT, "wmod": wmod, "ccol": ccol_,
            "gb_in": gb, "vb_in": vb, "ob_in": ob, "biastab": bt,
            "pos_col": pos, "identb": identb, "identf": identf,
        })
    return in_maps


def kernel(**inputs):
    in_maps = _host_prep(**inputs)
    nc, _ = build()
    res = run_bass_kernel_spmd(nc, in_maps, core_ids=list(range(8)))
    out = np.empty((B, T, C), dtype=np.float32)
    for core in range(8):
        b, half = core // 2, core % 2
        out[b, half * 1024:(half + 1) * 1024] = res.results[core]["y"]
    return out
